# version 05
# baseline (speedup 1.0000x reference)
import numpy as np
import ml_dtypes
import concourse.bass as bass
import concourse.mybir as mybir
from concourse.bass_utils import run_bass_kernel_spmd

F32 = mybir.dt.float32
BF16 = mybir.dt.bfloat16
AF = mybir.ActivationFunctionType
ALU = mybir.AluOpType
AX = mybir.AxisListType

NCORE = 8
S = 8192
D = 2048
NH = 16
DH = 128
LAT = 512
TOK = S // NCORE
NT = TOK // 128
GT = S // 128
DC = D // 128
LC = LAT // 128
FFN_H = 5632
HC = FFN_H // 128
DI = 3072
EC = DI // 128
LN_EPS = 1e-5
NEG = -1.0e30
QSCALE = float(1.0 / np.sqrt(128.0))


class Sem:
    __slots__ = ("h", "n")

    def __init__(self, nc):
        self.h = nc.alloc_semaphore()
        self.n = 0

    def inc(self, instr, amt=1):
        instr.then_inc(self.h, amt)
        self.n += amt
        return self.n

    def wait(self, eng, val=None):
        eng.wait_ge(self.h, self.n if val is None else val)


class K:

    def __init__(self, pool="default"):
        nc = bass.Bass("TRN2", target_bir_lowering=False, debug=False)
        self.nc = nc
        self.qs = Sem(nc)
        self.qp = Sem(nc)
        self.pe = Sem(nc)
        self.dv = Sem(nc)
        self.ac = Sem(nc)
        self.po = Sem(nc)
        self.bar = Sem(nc)
        if pool == "attn":
            self.pp = [nc.alloc_psum_tensor(f"pp{i}", [128, 512], F32)
                       for i in range(6)]
            self.pt_bf = nc.alloc_psum_tensor("pt_bf", [128, 512], BF16)
            self.pt_bf2 = nc.alloc_psum_tensor("pt_bf2", [128, 512], BF16)
            self.pt32 = None
        else:
            self.pp = [nc.alloc_psum_tensor(f"pp{i}", [128, 512], F32)
                       for i in range(6)]
            self.pt_bf = nc.alloc_psum_tensor("pt_bf", [128, 512], BF16)
            self.pt32 = nc.alloc_psum_tensor("pt32", [128, 128], F32)
        self.dummy_dv = nc.alloc_sbuf_tensor("dummy_dv", [1, 1], F32)
        self.dummy_ac = nc.alloc_sbuf_tensor("dummy_ac", [1, 1], F32)
        self.block_guard = nc.Block()
        self.block = self.block_guard.__enter__()

    def sbuf(self, name, shape, dtype):
        return self.nc.alloc_sbuf_tensor(name, shape, dtype)

    def arena(self, name, bytes_per_part):
        h = self.nc.alloc_sbuf_tensor(name, [128, bytes_per_part // 2], BF16)
        return self.nc.lookup_mloc(h).addr

    def sbuf_at(self, name, shape, dtype, offset):
        return self.nc.alloc_sbuf_tensor_at(name, shape, dtype, offset=offset)

    def dram(self, name, shape, dtype, kind="Internal"):
        return self.nc.dram_tensor(name, shape, dtype, kind=kind).ap()

    def finish(self):
        qs, qp = self.qs, self.qp
        blk = self.block

        @blk.sync
        def _(sy):
            qs.wait(sy)

        @blk.gpsimd
        def _(g):
            qp.wait(g)

        self.block_guard.__exit__(None, None, None)
        return self.nc

    def barrier(self):
        blk, bar = self.block, self.bar
        target = bar.n + 5
        qs, qp = self.qs, self.qp

        @blk.sync
        def _(sy):
            qs.wait(sy)
            sy.nop().then_inc(bar.h, 1)
            sy.wait_ge(bar.h, target)

        @blk.gpsimd
        def _(g):
            qp.wait(g)
            g.nop().then_inc(bar.h, 1)
            g.wait_ge(bar.h, target)

        @blk.vector
        def _(ve):
            ve.memset(self.dummy_dv[:], 0.0).then_inc(bar.h, 1)
            ve.wait_ge(bar.h, target)

        @blk.scalar
        def _(sc):
            sc.memzero(self.dummy_ac[:]).then_inc(bar.h, 1)
            sc.wait_ge(bar.h, target)

        @blk.tensor
        def _(t):
            t.nop().then_inc(bar.h, 1)
            t.wait_ge(bar.h, target)

        bar.n = target

    def emit_identity(self):
        self.ident = self.sbuf("ident_bf", [128, 128], BF16)
        self.ident32 = self.sbuf("ident_f32", [128, 128], F32)
        po = self.po

        @self.block.gpsimd
        def _(g):
            g.memset(self.ident[:], 1.0)
            po.inc(g.affine_select(self.ident[:], self.ident[:], [[-1, 128]],
                                   ALU.is_equal, 0.0, base=0,
                                   channel_multiplier=1))
            g.wait_ge(po.h, po.n)
            g.memset(self.ident32[:], 1.0)
            po.inc(g.affine_select(self.ident32[:], self.ident32[:], [[-1, 128]],
                                   ALU.is_equal, 0.0, base=0,
                                   channel_multiplier=1))

    def transpose_serial(self, pairs, dtype=BF16, width=128):
        blk, pe, ac = self.block, self.pe, self.ac
        pt = self.pt_bf if dtype == BF16 else self.pt32
        ident = self.ident if dtype == BF16 else self.ident32
        pe0, ac0 = pe.n, ac.n
        nslot = 4 if (dtype == BF16 and width <= 128) else 1
        groups = [pairs[i:i + nslot] for i in range(0, len(pairs), nslot)]

        @blk.tensor
        def _(t):
            for gi, grp in enumerate(groups):
                if gi >= 1:
                    t.wait_ge(ac.h, ac0 + gi)
                for si, (src, _dst) in enumerate(grp):
                    slot = si * width
                    p = src.shape[0]
                    i = t.transpose(pt[0:src.shape[-1], slot:slot + width],
                                    src, ident[0:p, 0:p])
                pe.inc(i)

        @blk.scalar
        def _(sc):
            for gi, grp in enumerate(groups):
                sc.wait_ge(pe.h, pe0 + gi + 1)
                for si, (_src, dst) in enumerate(grp):
                    slot = si * width
                    i = sc.copy(dst, pt[0:dst.shape[0], slot:slot + width])
                ac.inc(i)

    def bcast_rows(self, jobs):
        blk, pe, ac = self.block, self.pe, self.ac
        pb = self.pp[5]
        pe0, ac0 = pe.n, ac.n

        @blk.tensor
        def _(t):
            for i, (row, _dst) in enumerate(jobs):
                if i >= 1:
                    t.wait_ge(ac.h, ac0 + i)
                pe.inc(t.matmul(pb[:, 0:row.shape[-1]], self.ones_row32[:],
                                row, start=True, stop=True))

        @blk.scalar
        def _(sc):
            for i, (row, dst) in enumerate(jobs):
                sc.wait_ge(pe.h, pe0 + i + 1)
                ac.inc(sc.copy(dst, pb[:, 0:row.shape[-1]]))




def emit_pre(k: K, dbg=False, stop_after=None):
    nc, blk = k.nc, k.block
    x_full = k.dram("x_full", [S, D], F32, "ExternalInput")
    x_own = k.dram("x_own", [TOK, D], F32, "ExternalInput")
    wdkv = k.dram("wdkv", [LAT, D], F32, "ExternalInput")
    wuv = k.dram("wuv", [D, LAT], F32, "ExternalInput")
    gamma_bc = k.dram("gamma_bc", [128, LAT], F32, "ExternalInput")
    beta_bc = k.dram("beta_bc", [128, LAT], F32, "ExternalInput")

    okind = "ExternalOutput"
    v_dram = k.dram("v_dram", [GT, 128, D], BF16, okind if dbg else "Internal")
    ckv_out = k.dram("ckv_out", [LAT, S], BF16, okind)
    xn_out = k.dram("xn_out", [D, TOK], BF16, okind)

    qs, qp, pe, dv, ac, po = k.qs, k.qp, k.pe, k.dv, k.ac, k.po

    k.emit_identity()
    wnat = [k.sbuf(f"wnat{c}", [128, D], BF16) for c in range(LC)]
    wdkvT = [k.sbuf(f"wdkvT{kk}", [128, LAT], BF16) for kk in range(DC)]
    wuvT = [k.sbuf(f"wuvT{c}", [128, D], BF16) for c in range(LC)]
    gbc = k.sbuf("gbc", [128, LAT], F32)
    bbc = k.sbuf("bbc", [128, LAT], F32)

    @blk.gpsimd
    def _(g):
        for c in range(LC):
            qp.inc(g.dma_start(out=wnat[c][:],
                               in_=wdkv[c * 128:(c + 1) * 128, :]), 16)

    @blk.sync
    def _(sy):
        qs.inc(sy.dma_start(out=gbc[:], in_=gamma_bc), 16)
        qs.inc(sy.dma_start(out=bbc[:], in_=beta_bc), 16)

    k.barrier()
    k.transpose_serial(
        [(wnat[c][:, kk * 128:(kk + 1) * 128],
          wdkvT[kk][:, c * 128:(c + 1) * 128])
         for c in range(LC) for kk in range(DC)])
    k.barrier()

    @blk.gpsimd
    def _(g):
        for kk in range(DC):
            qp.inc(g.dma_start(
                out=wnat[kk // 4][:, (kk % 4) * 512:(kk % 4) * 512 + 512],
                in_=wuv[kk * 128:(kk + 1) * 128, :]), 16)

    k.barrier()
    k.transpose_serial(
        [(wnat[kk // 4][:, (kk % 4) * 512 + c * 128:
                        (kk % 4) * 512 + (c + 1) * 128],
          wuvT[c][:, kk * 128:(kk + 1) * 128])
         for kk in range(DC) for c in range(LC)])
    k.barrier()
    if stop_after == "consts":
        return None

    k.ckv = [k.sbuf(f"ckv{c}", [128, S], BF16) for c in range(LC)]
    xf = [k.sbuf(f"xf{i}", [128, D], F32) for i in range(2)]
    xn_tm = k.sbuf("xn_tm", [128, D], BF16)
    xnT = [k.sbuf(f"xnT{i}", [128, D], BF16) for i in range(2)]
    mv6 = k.sbuf("mv6", [128, 24], F32)
    mv2 = k.sbuf("mv2", [128, 2], F32)
    stat = k.sbuf("stat", [128, 2], F32)
    cstat = k.sbuf("cstat", [128, 2], F32)
    ctm = k.sbuf("ctm", [128, LAT], BF16)
    v_sb = [k.sbuf(f"v_sb{i}", [128, D], BF16) for i in range(2)]

    def tile_stats(ve, src_ap, dst_stat):
        for gq in range(4):
            dv.inc(ve.bn_stats(mv6[:, 6 * gq:6 * gq + 6],
                               src_ap[:, 512 * gq:512 * (gq + 1)]))
        ve.wait_ge(dv.h, dv.n)
        dv.inc(ve.bn_aggr(mv2[:], mv6[:]))
        ve.wait_ge(dv.h, dv.n)
        dv.inc(ve.tensor_scalar_add(dst_stat[:, 1:2], mv2[:, 1:2], LN_EPS))
        dv.inc(ve.tensor_copy(dst_stat[:, 0:1], mv2[:, 0:1]))
        ve.wait_ge(dv.h, dv.n)
        ve.wait_ge(ac.h, ac.n + 1)
        dv.inc(ve.reciprocal(dst_stat[:, 1:2], dst_stat[:, 1:2]))

    def tile_stats_sqrt(sc, dst_stat, dv_before):
        sc.wait_ge(dv.h, dv_before)
        ac.inc(sc.activation(dst_stat[:, 1:2], dst_stat[:, 1:2], AF.Sqrt))

    for g in range(GT):
        @blk.sync
        def _(sy, g=g):
            if g >= 2:
                sy.wait_ge(dv.h, k._xn_done[g - 2])
            qs.inc(sy.dma_start(out=xf[g % 2][:],
                                in_=x_full[g * 128:(g + 1) * 128, :]), 16)

        if g == 0:
            k._xn_done = {}

        qs_after_load = qs.n

        @blk.vector
        def _(ve, g=g, qs_after_load=qs_after_load):
            ve.wait_ge(qs.h, qs_after_load)
            tile_stats(ve, xf[g % 2][:], stat)
            ve.wait_ge(dv.h, dv.n)
            if g >= 1:
                ve.wait_ge(ac.h, k._xnT_evac[g - 1])
            dv.inc(ve.tensor_scalar(xn_tm[:], xf[g % 2][:], stat[:, 0:1],
                                    stat[:, 1:2], ALU.subtract, ALU.mult))
            k._xn_done[g] = dv.n

        @blk.scalar
        def _(sc, g=g):
            tile_stats_sqrt(sc, stat, dv.n - 2)

        if g == 0:
            k._xnT_evac = {}
        dv_xn = dv.n
        pe_t0, ac_t0 = pe.n, ac.n

        @blk.tensor
        def _(t, g=g, dv_xn=dv_xn, pe_t0=pe_t0, ac_t0=ac_t0):
            t.wait_ge(dv.h, dv_xn)
            for grp in range(4):
                if grp >= 1:
                    t.wait_ge(ac.h, ac_t0 + grp)
                for s4 in range(4):
                    kk = grp * 4 + s4
                    i = t.transpose(k.pt_bf[:, s4 * 128:(s4 + 1) * 128],
                                    xn_tm[:, kk * 128:(kk + 1) * 128], k.ident[:])
                pe.inc(i)

        @blk.scalar
        def _(sc, g=g, pe_t0=pe_t0):
            for grp in range(4):
                sc.wait_ge(pe.h, pe_t0 + grp + 1)
                ac.inc(sc.copy(xnT[g % 2][:, grp * 512:(grp + 1) * 512],
                               k.pt_bf[:]))
            k._xnT_evac[g] = ac.n

        ac_xnT = ac.n

        @blk.tensor
        def _(t, g=g, ac_xnT=ac_xnT):
            t.wait_ge(ac.h, ac_xnT)
            if g >= 1:
                t.wait_ge(dv.h, k._ctm_read[g - 1])
            for kk in range(DC):
                i = t.matmul(k.pp[0][:], xnT[g % 2][:, kk * 128:(kk + 1) * 128],
                             wdkvT[kk][:], start=(kk == 0), stop=(kk == DC - 1))
            pe.inc(i)

        pe_c = pe.n
        if g == 0:
            k._ctm_read = {}
            k._ckv_done = {}
            k._v_done = {}

        @blk.vector
        def _(ve, g=g, pe_c=pe_c):
            ve.wait_ge(pe.h, pe_c)
            for gq in range(1):
                dv.inc(ve.bn_stats(mv6[:, 0:6], k.pp[0][:, 0:512]))
            ve.wait_ge(dv.h, dv.n)
            dv.inc(ve.bn_aggr(mv2[:], mv6[:, 0:6]))
            ve.wait_ge(dv.h, dv.n)
            dv.inc(ve.tensor_scalar_add(cstat[:, 1:2], mv2[:, 1:2], LN_EPS))
            dv.inc(ve.tensor_copy(cstat[:, 0:1], mv2[:, 0:1]))
            ve.wait_ge(dv.h, dv.n)
            ve.wait_ge(ac.h, ac.n + 1)
            dv.inc(ve.reciprocal(cstat[:, 1:2], cstat[:, 1:2]))
            ve.wait_ge(dv.h, dv.n)
            if g >= 1:
                ve.wait_ge(ac.h, k._ckv_done[g - 1])
            dv.inc(ve.tensor_scalar(ctm[:], k.pp[0][:, 0:512], cstat[:, 0:1],
                                    cstat[:, 1:2], ALU.subtract, ALU.mult))
            ve.wait_ge(dv.h, dv.n)
            dv.inc(ve.tensor_mul(ctm[:], ctm[:], gbc[:]))
            ve.wait_ge(dv.h, dv.n)
            dv.inc(ve.tensor_add(ctm[:], ctm[:], bbc[:]))
            k._ctm_read[g] = dv.n

        @blk.scalar
        def _(sc, g=g):
            sc.wait_ge(dv.h, dv.n - 4)
            ac.inc(sc.activation(cstat[:, 1:2], cstat[:, 1:2], AF.Sqrt))


        dv_ctm = dv.n
        pe_ct0, ac_ct0 = pe.n, ac.n

        @blk.tensor
        def _(t, g=g, dv_ctm=dv_ctm):
            t.wait_ge(dv.h, dv_ctm)
            for s4 in range(4):
                i = t.transpose(k.pt_bf[:, s4 * 128:(s4 + 1) * 128],
                                ctm[:, s4 * 128:(s4 + 1) * 128], k.ident[:])
            pe.inc(i)

        pe_ctr = pe.n

        @blk.scalar
        def _(sc, g=g, pe_ctr=pe_ctr):
            sc.wait_ge(pe.h, pe_ctr)
            for c in range(LC):
                i = sc.copy(k.ckv[c][:, g * 128:(g + 1) * 128],
                            k.pt_bf[:, c * 128:(c + 1) * 128])
            ac.inc(i)
            k._ckv_done[g] = ac.n

        ac_ckv = ac.n

        @blk.tensor
        def _(t, g=g, ac_ckv=ac_ckv):
            t.wait_ge(ac.h, ac_ckv)
            if g >= 1:
                t.wait_ge(ac.h, k._v_evac[g - 1])
            for c in range(LC):
                for dg in range(4):
                    i = t.matmul(k.pp[1 + dg][:],
                                 k.ckv[c][:, g * 128:(g + 1) * 128],
                                 wuvT[c][:, dg * 512:(dg + 1) * 512],
                                 start=(c == 0), stop=(c == LC - 1))
            pe.inc(i)
            k._v_done[g] = pe.n

        pe_v = pe.n
        if g == 0:
            k._v_evac = {}

        @blk.scalar
        def _(sc, g=g, pe_v=pe_v):
            sc.wait_ge(pe.h, pe_v)
            if g >= 2:
                sc.wait_ge(qs.h, k._v_dma[g - 2])
            for dg in range(4):
                i = sc.copy(v_sb[g % 2][:, dg * 512:(dg + 1) * 512],
                            k.pp[1 + dg][:])
            ac.inc(i)
            k._v_evac[g] = ac.n

        ac_v = ac.n
        if g == 0:
            k._v_dma = {}

        @blk.sync
        def _(sy, g=g, ac_v=ac_v):
            sy.wait_ge(ac.h, ac_v)
            qs.inc(sy.dma_start(out=v_dram[g], in_=v_sb[g % 2][:]), 16)
            k._v_dma[g] = qs.n

    k.barrier()

    @blk.sync
    def _(sy):
        for c in range(LC):
            qs.inc(sy.dma_start(out=ckv_out[c * 128:(c + 1) * 128, :],
                                in_=k.ckv[c][:]), 16)

    if stop_after == "ckv":
        k.barrier()
        return None

    k.xn = [k.sbuf(f"xn{kk}", [128, TOK], BF16) for kk in range(DC)]
    for j in range(NT):
        @blk.sync
        def _(sy, j=j):
            if j >= 2:
                sy.wait_ge(dv.h, k._xn_done[GT + j - 2])
            qs.inc(sy.dma_start(out=xf[j % 2][:],
                                in_=x_own[j * 128:(j + 1) * 128, :]), 16)

        qs_l = qs.n

        @blk.vector
        def _(ve, j=j, qs_l=qs_l):
            ve.wait_ge(qs.h, qs_l)
            tile_stats(ve, xf[j % 2][:], stat)
            ve.wait_ge(dv.h, dv.n)
            if j >= 1:
                ve.wait_ge(ac.h, k._xnT_evac[GT + j - 1])
            dv.inc(ve.tensor_scalar(xn_tm[:], xf[j % 2][:], stat[:, 0:1],
                                    stat[:, 1:2], ALU.subtract, ALU.mult))
            k._xn_done[GT + j] = dv.n

        @blk.scalar
        def _(sc, j=j):
            tile_stats_sqrt(sc, stat, dv.n - 2)

        dv_xn = dv.n
        pe_t0, ac_t0 = pe.n, ac.n

        @blk.tensor
        def _(t, j=j, dv_xn=dv_xn, pe_t0=pe_t0, ac_t0=ac_t0):
            t.wait_ge(dv.h, dv_xn)
            for grp in range(4):
                if grp >= 1:
                    t.wait_ge(ac.h, ac_t0 + grp)
                for s4 in range(4):
                    kk = grp * 4 + s4
                    i = t.transpose(k.pt_bf[:, s4 * 128:(s4 + 1) * 128],
                                    xn_tm[:, kk * 128:(kk + 1) * 128], k.ident[:])
                pe.inc(i)

        @blk.scalar
        def _(sc, j=j, pe_t0=pe_t0):
            for grp in range(4):
                sc.wait_ge(pe.h, pe_t0 + grp + 1)
                for s4 in range(4):
                    kk = grp * 4 + s4
                    i = sc.copy(k.xn[kk][:, j * 128:(j + 1) * 128],
                                k.pt_bf[:, s4 * 128:(s4 + 1) * 128])
                ac.inc(i)
            k._xnT_evac[GT + j] = ac.n

    k.barrier()

    @blk.sync
    def _(sy):
        for kk in range(DC):
            qs.inc(sy.dma_start(out=xn_out[kk * 128:(kk + 1) * 128, :],
                                in_=k.xn[kk][:]), 16)

    k.barrier()
    return None


def emit_attn(k: K, standalone=True, stop_after=None):
    nc, blk = k.nc, k.block
    qs, qp, pe, dv, ac, po = k.qs, k.qp, k.pe, k.dv, k.ac, k.po
    s_sc = Sem(nc)
    s_pt = Sem(nc)
    s_vu = Sem(nc)
    s_cx = Sem(nc)
    s_ex = Sem(nc)

    wq = k.dram("wq", [D, D], F32, "ExternalInput")
    wuk = k.dram("wuk", [D, LAT], F32, "ExternalInput")
    wo = k.dram("wo", [D, D], F32, "ExternalInput")
    maskin = k.dram("maskin", [128, 1024], F32, "ExternalInput")
    satt_out = k.dram("satt_out", [D, TOK], BF16, "ExternalOutput")
    if standalone:
        v_dram = k.dram("v_dram", [GT, 128, D], BF16, "ExternalInput")
        xn_in = k.dram("xn_in", [D, TOK], BF16, "ExternalInput")
        ckv_in = k.dram("ckv_in", [LAT, S], BF16, "ExternalInput")
        k.emit_identity()
        k.ckv = [k.sbuf(f"ckv{c}", [128, S], BF16) for c in range(LC)]

        @blk.sync
        def _(sy):
            for c in range(LC):
                qs.inc(sy.dma_start(out=k.ckv[c][:],
                                    in_=ckv_in[c * 128:(c + 1) * 128, :]), 16)
        k.barrier()
    else:
        v_dram = k.v_dram

    ar = k.arena("attn_arena", 49152)
    mask = k.sbuf("attn_mask", [128, 1024], F32)
    wuk_sb = [k.sbuf_at(f"wuk{kk}", [128, LAT], BF16, ar + 1024 * kk)
              for kk in range(DC)]
    wstg = [k.sbuf_at(f"attn_wstg{i}", [128, D], BF16, ar + 16384 + 4096 * i)
            for i in range(2)]
    wqt = k.sbuf_at("attn_wqt", [128, D], BF16, ar + 24576)
    ak = [k.sbuf(f"ak{h}", [128, LAT], BF16) for h in range(NH)]

    @blk.sync
    def _(sy):
        qs.inc(sy.dma_start(out=mask[:], in_=maskin), 16)

    @blk.gpsimd
    def _(g):
        for kk in range(DC):
            qp.inc(g.dma_start(out=wuk_sb[kk][:],
                               in_=wuk[kk * 128:(kk + 1) * 128, :]), 16)
        for h in range(2):
            qp.inc(g.dma_start(out=wstg[h][:],
                               in_=wq[h * 128:(h + 1) * 128, :]), 16)

    k.barrier()

    ak_ev = {}
    for h in range(NH):
        pe_t0, ac_t0 = pe.n, ac.n

        @blk.tensor
        def _(t, h=h, pe_t0=pe_t0, ac_t0=ac_t0):
            if h >= 2:
                t.wait_ge(qp.h, k._wq_pf[h - 2])
            for grp in range(4):
                if grp >= 1:
                    t.wait_ge(ac.h, ac_t0 + grp)
                for s4 in range(4):
                    kk = grp * 4 + s4
                    i = t.transpose(k.pt_bf[:, s4 * 128:(s4 + 1) * 128],
                                    wstg[h % 2][:, kk * 128:(kk + 1) * 128],
                                    k.ident[:])
                pe.inc(i)

        @blk.scalar
        def _(sc, h=h, pe_t0=pe_t0):
            for grp in range(4):
                sc.wait_ge(pe.h, pe_t0 + grp + 1)
                ac.inc(sc.copy(wqt[:, grp * 512:(grp + 1) * 512], k.pt_bf[:]))

        ac_wqt = ac.n

        @blk.tensor
        def _(t, h=h, ac_wqt=ac_wqt):
            t.wait_ge(ac.h, ac_wqt)
            if h >= 1:
                t.wait_ge(ac.h, ak_ev[h - 1])
            for kk in range(DC):
                i = t.matmul(k.pp[5][:], wqt[:, kk * 128:(kk + 1) * 128],
                             wuk_sb[kk][:], start=(kk == 0), stop=(kk == DC - 1))
            pe.inc(i)

        pe_ak = pe.n

        @blk.scalar
        def _(sc, h=h, pe_ak=pe_ak):
            sc.wait_ge(pe.h, pe_ak)
            ac.inc(sc.copy(ak[h][:], k.pp[5][:]))
        ak_ev[h] = ac.n

        if h == 0:
            k._wq_pf = {}

        @blk.gpsimd
        def _(g, h=h, pe_t0=pe_t0):
            if h + 2 < NH:
                g.wait_ge(pe.h, pe_t0 + 4)
                qp.inc(g.dma_start(out=wstg[h % 2][:],
                                   in_=wq[(h + 2) * 128:(h + 3) * 128, :]), 16)
                k._wq_pf[h] = qp.n

    k.barrier()
    if stop_after == "ak":
        ak_dbg = k.dram("ak_dbg", [NH, 128, LAT], BF16, "ExternalOutput")

        @blk.sync
        def _(sy):
            for h in range(NH):
                qs.inc(sy.dma_start(out=ak_dbg[h], in_=ak[h][:]), 16)
        k.barrier()
        return None

    scores = k.sbuf_at("scores", [128, S], F32, ar)
    pstrip = k.sbuf_at("pstrip", [128, S], BF16, ar + 32768)
    tmpTh = [k.sbuf(f"tmpTh{h}", [128, LAT], BF16) for h in range(NH)]
    xnq = [k.sbuf(f"xnq{i}", [128, D], BF16) for i in range(2)]
    vring = [k.sbuf(f"vring{i}", [128, 1024], BF16) for i in range(3)]
    pTst = [k.sbuf(f"pTst{i}", [128, 512], BF16) for i in range(4)]
    ctx_fm = [k.sbuf(f"ctx{kk}", [128, TOK], BF16) for kk in range(DC)]
    mrow = k.sbuf("attn_mrow", [128, 1], F32)
    negm = k.sbuf("attn_negm", [128, 1], F32)
    ell = [k.sbuf(f"attn_ell{i}", [128, 1], F32) for i in range(2)]
    rinv = [k.sbuf(f"attn_rinv{i}", [128, 1], F32) for i in range(2)]
    ctx_tm = k.sbuf("ctx_tm", [128, 128], BF16)

    s_tq = Sem(nc)
    s_ptA = Sem(nc)
    s_ptD = Sem(nc)
    s_ptr = Sem(nc)
    s_cm = Sem(nc)
    s_xq = Sem(nc)

    sc_mm = {}
    exp_done = {}
    cm_end = {}
    vdma = {}
    vfree = {}
    gvc_counter = [0]
    hidx = [0]

    heads = [(j, h) for j in range(NT) for h in range(NH)]
    only_batch = stop_after == "h0"
    only_qk = stop_after == "hq"
    if only_batch or only_qk:
        heads = heads[:1]
        stop_after = "ctx"
    elif stop_after and stop_after.startswith("h") and stop_after[1:].isdigit():
        heads = heads[:int(stop_after[1:])]
        stop_after = "ctx"
    cm_expect = {}
    acc = 0
    for (j, h) in heads:
        acc += 2 * j + 2
        cm_expect[(j, h)] = acc
    tq_expect = {}

    def emit_tmpT_batch(j):
        @blk.sync
        def _(sy, j=j):
            if j >= 2:
                sy.wait_ge(s_xq.h, j - 1)
            for kk in range(DC):
                qs.inc(sy.dma_start(
                    out=xnq[j % 2][:, kk * 128:(kk + 1) * 128],
                    in_=xn_in[kk * 128:(kk + 1) * 128,
                              j * 128:(j + 1) * 128]), 16)

        qs_xn = qs.n
        sc_prev = s_sc.n
        tq0 = s_tq.n

        @blk.tensor
        def _(t, j=j, qs_xn=qs_xn, sc_prev=sc_prev, tq0=tq0):
            t.wait_ge(qs.h, qs_xn)
            t.wait_ge(s_sc.h, sc_prev)
            i = None
            for r4 in range(4):
                for hi in range(4):
                    h = r4 * 4 + hi
                    if r4 >= 1:
                        t.wait_ge(s_tq.h, tq0 + 4 * (r4 - 1) + hi + 1)
                    for ls in range(LC):
                        i = t.matmul(k.pp[hi][:, ls * 128:(ls + 1) * 128],
                                     ak[h][:, ls * 128:(ls + 1) * 128],
                                     xnq[j % 2][:, h * 128:(h + 1) * 128],
                                     start=(ls == 0), stop=(ls == LC - 1))
                    s_ptr.inc(i)
            s_xq.inc(t.nop())

        ptr_b = s_ptr.n - 16

        @blk.vector
        def _(ve, j=j, ptr_b=ptr_b):
            for r4 in range(4):
                for hi in range(4):
                    h = r4 * 4 + hi
                    ve.wait_ge(s_ptr.h, ptr_b + r4 * 4 + hi + 1)
                    i = ve.tensor_copy(tmpTh[h][:], k.pp[hi][:])
                    s_tq.inc(i)
        tq_expect[j] = s_tq.n

    def emit_qk(j, h, prev):
        L = (8 * j + 8) * 128
        nkg = L // 512
        hj = (j, h)

        sc_base = s_sc.n

        @blk.tensor
        def _(t, j=j, h=h, nkg=nkg, sc_base=sc_base, prev=prev):
            t.wait_ge(s_tq.h, tq_expect[j])
            for kg in range(nkg):
                pb = k.pp[kg % 4]
                if kg >= 4:
                    t.wait_ge(s_sc.h, sc_base + kg - 3)
                elif prev is not None and kg == 0:
                    t.wait_ge(s_sc.h, sc_base)
                for ls in range(LC):
                    i = t.matmul(pb[:],
                                 tmpTh[h][:, ls * 128:(ls + 1) * 128],
                                 k.ckv[ls][:, kg * 512:(kg + 1) * 512],
                                 start=(ls == 0), stop=(ls == LC - 1))
                pe.inc(i)
                sc_mm[(j, h, kg)] = pe.n

        @blk.vector
        def _(ve, j=j, h=h, nkg=nkg, L=L, prev=prev):
            if prev is not None:
                ve.wait_ge(s_ex.h, exp_done[prev])
            for kg in range(nkg):
                pb = k.pp[kg % 4]
                ve.wait_ge(pe.h, sc_mm[(j, h, kg)])
                if kg >= nkg - 2:
                    moff = (kg - (nkg - 2)) * 512
                    i = ve.scalar_tensor_tensor(
                        scores[:, kg * 512:(kg + 1) * 512], pb[:], 1.0,
                        mask[:, moff:moff + 512], ALU.mult, ALU.add)
                else:
                    i = ve.tensor_copy(scores[:, kg * 512:(kg + 1) * 512],
                                       pb[:])
                s_sc.inc(i)

        def _max_part(j=j, h=h, L=L, ssc=s_sc.n):
            @blk.vector
            def _(ve):
                ve.wait_ge(s_sc.h, ssc)
                dv.inc(ve.reduce_max(mrow[:], scores[:, 0:L], axis=AX.X))
                ve.wait_ge(dv.h, dv.n)
                dv.inc(ve.tensor_scalar_mul(negm[:], mrow[:], -QSCALE))
            return dv.n

        def _exp_part(dvn, j=j, h=h, L=L, prev=prev, hj=hj):
            @blk.scalar
            def _(sc):
                sc.wait_ge(dv.h, dvn)
                if prev is not None:
                    sc.wait_ge(s_cm.h, cm_expect[prev])
                i = sc.activation(pstrip[:, 0:L], scores[:, 0:L], AF.Exp,
                                  bias=negm[:], scale=QSCALE,
                                  accum_out=ell[h % 2][:])
                s_ex.inc(i)
            exp_done[hj] = s_ex.n
        return _max_part, _exp_part

    def emit_ctx(j, h):
        L = (8 * j + 8) * 128
        nkt = L // 128
        nvc = L // 1024
        nkq = nkt // 4
        hj = (j, h)

        gvc0 = gvc_counter[0]
        cm_base = s_cm.n
        for vc in range(nvc):
            vfree[gvc0 + vc] = cm_base + 2 * vc + 2

        @blk.sync
        def _(sy, j=j, h=h, nvc=nvc, gvc0=gvc0):
            for vc in range(nvc):
                gvc = gvc0 + vc
                if gvc >= 3:
                    sy.wait_ge(s_cm.h, vfree[gvc - 3])
                qs.inc(sy.dma_start(
                    out=vring[gvc % 3][:].rearrange("p (kt d) -> p kt d",
                                                    d=128),
                    in_=v_dram[8 * vc:8 * vc + 8, :, h * 128:(h + 1) * 128]
                    .rearrange("kt p d -> p kt d")), 16)
                vdma[gvc] = qs.n

        ptA_base = s_ptA.n
        ptD_base = s_ptD.n
        ptr_base = s_ptr.n

        @blk.tensor
        def _(t, j=j, h=h, nkq=nkq, nkt=nkt, gvc0=gvc0,
              ptA=ptA_base, ptD=ptD_base, sxn=exp_done[hj], hx=hidx[0]):
            t.wait_ge(s_ex.h, sxn)
            for kq in range(nkq):
                even = (kq % 2 == 0)
                bank = k.pt_bf if even else k.pt_bf2
                sem, base = (s_ptA, ptA) if even else (s_ptD, ptD)
                if kq <= 1:
                    t.wait_ge(sem.h, base)
                else:
                    t.wait_ge(sem.h, base + kq // 2)
                for s4 in range(4):
                    kt = kq * 4 + s4
                    i = t.transpose(bank[:, s4 * 128:(s4 + 1) * 128],
                                    pstrip[:, kt * 128:(kt + 1) * 128],
                                    k.ident[:])
                s_ptr.inc(i)
                t.wait_ge(sem.h, base + kq // 2 + 1)
                t.wait_ge(qs.h, vdma[gvc0 + (kq * 4) // 8])
                if kq == 0:
                    t.wait_ge(s_cx.h, max(hx - 1, 0))
                cb = k.pp[4 + hx % 2]
                for s4 in range(4):
                    kt = kq * 4 + s4
                    vseg = vring[(gvc0 + kt // 8) % 3][:].rearrange(
                        "p (kt d) -> p kt d", d=128)[:, kt % 8, :]
                    i = t.matmul(cb[:, 0:128],
                                 pTst[kq % 4][:, s4 * 128:(s4 + 1) * 128],
                                 vseg, start=(kt == 0), stop=(kt == nkt - 1))
                s_cm.inc(i)
        assert s_cm.n == cm_expect[hj], (hj, s_cm.n, cm_expect[hj])
        gvc_counter[0] += nvc

        @blk.scalar
        def _(sc, nkq=nkq, ptr_base=ptr_base, cm_base=cm_base):
            for kq in range(0, nkq, 2):
                sc.wait_ge(s_ptr.h, ptr_base + kq + 1)
                if kq >= 4:
                    sc.wait_ge(s_cm.h, cm_base + kq - 3)
                i = sc.copy(pTst[kq % 4][:], k.pt_bf[:])
                s_ptA.inc(i)

        @blk.vector
        def _(ve, h=h, nkq=nkq, ptr_base=ptr_base, cm_base=cm_base,
              sxn=exp_done[hj]):
            ve.wait_ge(s_ex.h, sxn)
            dv.inc(ve.reciprocal(rinv[h % 2][:], ell[h % 2][:]))
            for kq in range(1, nkq, 2):
                ve.wait_ge(s_ptr.h, ptr_base + kq + 1)
                if kq >= 4:
                    ve.wait_ge(s_cm.h, cm_base + kq - 3)
                i = ve.tensor_copy(pTst[kq % 4][:], k.pt_bf2[:])
                s_ptD.inc(i)
        dv_rinv = dv.n

        @blk.scalar
        def _(sc, h=h, cme=cm_expect[hj], dvr=dv_rinv):
            sc.wait_ge(s_cm.h, cme)
            sc.wait_ge(dv.h, dvr)
            i = sc.activation(ctx_tm[:], k.pp[4 + (hidx[0]) % 2][:, 0:128], AF.Copy,
                              scale=rinv[h % 2][:])
            s_cx.inc(i)
        hidx[0] += 1

        @blk.tensor
        def _(t, hx=hidx[0], spn=s_ptA.n):
            t.wait_ge(s_cx.h, hx)
            t.wait_ge(s_ptA.h, spn)
            i = t.transpose(k.pt_bf[:, 0:128], ctx_tm[:], k.ident[:])
            s_ptr.inc(i)

        @blk.scalar
        def _(sc, j=j, h=h, sptr=s_ptr.n):
            sc.wait_ge(s_ptr.h, sptr)
            i = sc.copy(ctx_fm[h][:, j * 128:(j + 1) * 128],
                        k.pt_bf[:, 0:128])
            s_ptA.inc(i)

    prev = None
    pending = None
    for hj in heads:
        if hj[1] == 0:
            emit_tmpT_batch(hj[0])
        if only_batch:
            break
        max_part, exp_part = emit_qk(hj[0], hj[1], prev)
        if pending is not None:
            emit_ctx(*pending)
        dvn = max_part()
        exp_part(dvn)
        prev = hj
        pending = hj
    if pending is not None and not only_qk:
        emit_ctx(*pending)

    k.barrier()
    if stop_after == "ctx":
        ctx_dbg = k.dram("ctx_dbg", [D, TOK], BF16, "ExternalOutput")

        @blk.sync
        def _(sy):
            for kk in range(DC):
                qs.inc(sy.dma_start(out=ctx_dbg[kk * 128:(kk + 1) * 128, :],
                                    in_=ctx_fm[kk][:]), 16)
        k.barrier()
        return None

    wot = wqt
    satt_sb = [k.sbuf_at(f"satt{i}", [128, TOK], BF16, ar + 28672 + 2048 * i)
               for i in range(2)]
    wo_pf = {}

    @blk.gpsimd
    def _(g):
        for oc in range(2):
            qp.inc(g.dma_start(out=wstg[oc][:],
                               in_=wo[oc * 128:(oc + 1) * 128, :]), 16)
            wo_pf[oc] = qp.n

    for oc in range(DC):
        pe_t0, ac_t0 = pe.n, ac.n

        @blk.tensor
        def _(t, oc=oc, pe_t0=pe_t0, ac_t0=ac_t0):
            t.wait_ge(qp.h, wo_pf[oc])
            for grp in range(4):
                if grp >= 1:
                    t.wait_ge(ac.h, ac_t0 + grp)
                for s4 in range(4):
                    kk = grp * 4 + s4
                    i = t.transpose(k.pt_bf[:, s4 * 128:(s4 + 1) * 128],
                                    wstg[oc % 2][:, kk * 128:(kk + 1) * 128],
                                    k.ident[:])
                pe.inc(i)

        @blk.scalar
        def _(sc, oc=oc, pe_t0=pe_t0):
            for grp in range(4):
                sc.wait_ge(pe.h, pe_t0 + grp + 1)
                ac.inc(sc.copy(wot[:, grp * 512:(grp + 1) * 512], k.pt_bf[:]))

        ac_wot = ac.n

        @blk.gpsimd
        def _(g, oc=oc, pe_t0=pe_t0):
            if oc + 2 < DC:
                g.wait_ge(pe.h, pe_t0 + 4)
                qp.inc(g.dma_start(out=wstg[oc % 2][:],
                                   in_=wo[(oc + 2) * 128:(oc + 3) * 128, :]), 16)
                wo_pf[oc + 2] = qp.n

        @blk.tensor
        def _(t, oc=oc, ac_wot=ac_wot):
            t.wait_ge(ac.h, ac_wot)
            if oc >= 1:
                t.wait_ge(ac.h, satt_ev[oc - 1])
            for qg in range(2):
                for kk in range(DC):
                    i = t.matmul(k.pp[qg][:],
                                 wot[:, kk * 128:(kk + 1) * 128],
                                 ctx_fm[kk][:, qg * 512:(qg + 1) * 512],
                                 start=(kk == 0), stop=(kk == DC - 1))
                pe.inc(i)

        pe_so = pe.n
        if oc == 0:
            satt_ev = {}

        @blk.scalar
        def _(sc, oc=oc, pe_so=pe_so):
            for qg in range(2):
                sc.wait_ge(pe.h, pe_so - 2 + qg + 1)
                if oc >= 2:
                    sc.wait_ge(qs.h, satt_dma[oc - 2])
                ac.inc(sc.copy(satt_sb[oc % 2][:, qg * 512:(qg + 1) * 512],
                               k.pp[qg][:]))
        satt_ev[oc] = ac.n
        ac_so = ac.n
        if oc == 0:
            satt_dma = {}

        @blk.sync
        def _(sy, oc=oc, ac_so=ac_so):
            sy.wait_ge(ac.h, ac_so)
            qs.inc(sy.dma_start(out=satt_out[oc * 128:(oc + 1) * 128, :],
                                in_=satt_sb[oc % 2][:]), 16)
        satt_dma[oc] = qs.n

    k.barrier()
    return None


def emit_ffn(k: K, standalone=True, stop_after=None):
    nc, blk = k.nc, k.block
    qs, qp, pe, dv, ac, po = k.qs, k.qp, k.pe, k.dv, k.ac, k.po
    s_q = Sem(nc)
    s_tr = Sem(nc)
    s_ev = Sem(nc)
    s_hd = Sem(nc)
    s_gf = Sem(nc)

    w1 = k.dram("w1", [FFN_H, D], F32, "ExternalInput")
    w2 = k.dram("w2", [FFN_H, D], F32, "ExternalInput")
    w3 = k.dram("w3", [D, FFN_H], F32, "ExternalInput")
    x_ownT = k.dram("x_ownT", [D, TOK], F32, "ExternalInput")
    x1_out = k.dram("x1_out", [D, TOK], F32, "ExternalOutput")
    if standalone:
        xn_in = k.dram("xn_in", [D, TOK], BF16, "ExternalInput")
        satt_in = k.dram("satt_in", [D, TOK], BF16, "ExternalInput")
        k.emit_identity()
        k.xn = [k.sbuf(f"xn{kk}", [128, TOK], BF16) for kk in range(DC)]

        @blk.sync
        def _(sy):
            for kk in range(DC):
                qs.inc(sy.dma_start(out=k.xn[kk][:],
                                    in_=xn_in[kk * 128:(kk + 1) * 128, :]), 16)
        k.barrier()
    else:
        satt_in = k.satt_dram

    wstg = [k.sbuf(f"ffn_wstg{i}", [128, D], F32) for i in range(4)]
    gcols = k.sbuf("ffn_gcols", [128, 136], F32)
    qa = k.sbuf("ffn_qa", [128, D], F32)
    ar2 = k.arena("ffn_qarena", 24576)
    qbf = [k.sbuf_at(f"ffn_qbf{i}", [128, D], BF16, ar2 + 4096 * i)
           for i in range(2)]
    qT = [k.sbuf_at(f"ffn_qT{i}", [128, D], BF16, ar2 + 8192 + 4096 * i)
          for i in range(2)]
    hidden = [k.sbuf(f"hid{hc}", [128, TOK], BF16) for hc in range(HC)]
    gfm = [k.sbuf_at(f"gfm{kk}", [128, TOK], BF16,
                     k.nc.lookup_mloc(k.xn[kk]).addr) for kk in range(DC)]
    tau = [k.sbuf(f"tau{i}", [128, 1], F32) for i in range(3)]
    ntau = [k.sbuf(f"ntau{i}", [128, 1], F32) for i in range(3)]
    grow = k.sbuf("ffn_grow", [1, 128], F32)
    g11 = k.sbuf("ffn_g11", [1, 1], F32)
    sgt = k.sbuf("ffn_sgt", [128, TOK], F32)
    qbf3 = k.sbuf_at("ffn_qbf3", [128, FFN_H], BF16, ar2)
    q3T = k.sbuf_at("ffn_q3T", [128, FFN_H], BF16, ar2 + 11264)
    xstg = [k.sbuf(f"ffn_xstg{i}", [128, TOK], F32) for i in range(2)]
    sstg = [k.sbuf(f"ffn_sstg{i}", [128, TOK], BF16) for i in range(2)]

    jobs = []
    for hc in range(HC):
        jobs.append((w1[hc * 128:(hc + 1) * 128, :], len(jobs)))
    for hc in range(HC):
        jobs.append((w2[hc * 128:(hc + 1) * 128, :], len(jobs)))
    w3cols = [(0, 2048), (2048, 4096), (4096, 5632)]
    for dc in range(DC):
        for (c0, c1) in w3cols:
            jobs.append((w3[dc * 128:(dc + 1) * 128, c0:c1], len(jobs)))

    qs_g0, qp_g0, ac_g0 = qs.n, qp.n, ac.n

    @blk.sync
    def _(sy):
        for ji, (src, ci) in enumerate(jobs):
            if ji % 2 == 0:
                if ji >= 4:
                    sy.wait_ge(ac.h, ac_g0 + ji - 3)
                w = src.shape[-1]
                qs.inc(sy.dma_start(out=wstg[ji % 4][:, 0:w], in_=src), 16)

    @blk.gpsimd
    def _(g):
        for ji, (src, ci) in enumerate(jobs):
            if ji % 2 == 1:
                if ji >= 4:
                    g.wait_ge(ac.h, ac_g0 + ji - 3)
                w = src.shape[-1]
                qp.inc(g.dma_start(out=wstg[ji % 4][:, 0:w], in_=src), 16)

    @blk.scalar
    def _(sc):
        nq = {0: qs_g0, 1: qp_g0}
        seen = {0: 0, 1: 0}
        for ji, (src, ci) in enumerate(jobs):
            par = ji % 2
            seen[par] += 1
            sc.wait_ge([qs.h, qp.h][par], [qs_g0, qp_g0][par] + 16 * seen[par])
            w = src.shape[-1]
            ac.inc(sc.activation(qa[:, 0:w], wstg[ji % 4][:, 0:w], AF.Abs,
                                 accum_out=gcols[:, ci:ci + 1]))

    k.barrier()
    if stop_after == "gstream":
        gdbg2 = k.dram("gcols_dbg", [128, 136], F32, "ExternalOutput")

        @blk.sync
        def _(sy):
            qs.inc(sy.dma_start(out=gdbg2, in_=gcols[:]), 16)
        k.barrier()
        return None

    NW = float(FFN_H * D)
    ranges = [(0, HC), (HC, 2 * HC), (2 * HC, 2 * HC + 3 * DC)]
    for mi, (c0, c1) in enumerate(ranges):
        @blk.vector
        def _(ve, mi=mi, c0=c0, c1=c1):
            dv.inc(ve.reduce_sum(tau[mi][:], gcols[:, c0:c1], axis=AX.X))

        k.barrier()

        @blk.tensor
        def _(t, mi=mi):
            pe.inc(t.transpose(k.pt32[0:1, 0:128], tau[mi][:],
                               k.ident32[:]))

        @blk.scalar
        def _(sc, mi=mi):
            pe.wait(sc)
            ac.inc(sc.copy(grow[:], k.pt32[0:1, 0:128]))

        k.barrier()

        @blk.vector
        def _(ve, mi=mi):
            dv.inc(ve.reduce_sum(g11[:], grow[:], axis=AX.X))
            ve.wait_ge(dv.h, dv.n)
            s0 = bass.AP(tensor=g11.ap().tensor, offset=0, ap=[[1, 1], [0, 128]])
            dv.inc(ve.tensor_copy(qa[0:1, 0:128], s0))

        k.barrier()

        @blk.tensor
        def _(t, mi=mi):
            pe.inc(t.transpose(k.pt32[:], qa[:, 0:128], k.ident32[:]))

        @blk.scalar
        def _(sc, mi=mi):
            pe.wait(sc)
            ac.inc(sc.copy(tau[mi][:], k.pt32[:, 0:1]))

        k.barrier()

        @blk.vector
        def _(ve, mi=mi):
            dv.inc(ve.tensor_scalar(tau[mi][:], tau[mi][:], 0.5 / NW, 0.5e-8,
                                    ALU.mult, ALU.add))
            ve.wait_ge(dv.h, dv.n)
            dv.inc(ve.tensor_scalar_mul(ntau[mi][:], tau[mi][:], -1.0))

    k.barrier()
    if stop_after == "gamma":
        gdbg = k.dram("gamma_dbg", [3, 128, 1], F32, "ExternalOutput")

        @blk.sync
        def _(sy):
            for mi in range(3):
                qs.inc(sy.dma_start(out=gdbg[mi], in_=tau[mi][:]), 16)
        k.barrier()
        return None

    ld_w = {}
    for hc in range(HC):
        @blk.sync
        def _(sy, hc=hc):
            if hc >= 2:
                sy.wait_ge(s_q.h, 2 * (hc - 1))
            qs.inc(sy.dma_start(out=wstg[(2 * hc) % 4][:],
                                in_=w1[hc * 128:(hc + 1) * 128, :]), 16)
            ld_w[(hc, 0)] = qs.n

        @blk.gpsimd
        def _(g, hc=hc):
            if hc >= 2:
                g.wait_ge(s_q.h, 2 * (hc - 1))
            qp.inc(g.dma_start(out=wstg[(2 * hc + 1) % 4][:],
                               in_=w2[hc * 128:(hc + 1) * 128, :]), 16)
            ld_w[(hc, 1)] = qp.n

        for wi in range(2):
            @blk.vector
            def _(ve, hc=hc, wi=wi):
                ve.wait_ge([qs.h, qp.h][wi], ld_w[(hc, wi)])
                if hc >= 1:
                    ve.wait_ge(s_ev.h, 4 * (2 * (hc - 1) + wi + 1))
                stg = wstg[(2 * hc + wi) % 4]
                i = ve.tensor_scalar(qa[:], stg[:], ntau[wi][:], None, ALU.is_lt)
                dv.inc(i)
                ve.wait_ge(dv.h, dv.n)
                i = ve.scalar_tensor_tensor(qbf[wi][:], stg[:], tau[wi][:],
                                            qa[:], ALU.is_gt, ALU.subtract)
                s_q.inc(i)

            sq_now = s_q.n
            tr0 = s_tr.n
            ev0 = s_ev.n

            @blk.tensor
            def _(t, hc=hc, wi=wi, sq_now=sq_now, tr0=tr0, ev0=ev0):
                t.wait_ge(s_q.h, sq_now)
                for grp in range(4):
                    t.wait_ge(s_ev.h, ev0 + grp)
                    for s4 in range(4):
                        kk = grp * 4 + s4
                        i = t.transpose(k.pt_bf[:, s4 * 128:(s4 + 1) * 128],
                                        qbf[wi][:, kk * 128:(kk + 1) * 128],
                                        k.ident[:])
                    s_tr.inc(i)

            @blk.scalar
            def _(sc, hc=hc, wi=wi, tr0=tr0):
                for grp in range(4):
                    sc.wait_ge(s_tr.h, tr0 + grp + 1)
                    i = sc.copy(qT[wi][:, grp * 512:(grp + 1) * 512], k.pt_bf[:])
                    s_ev.inc(i)

            ev_end = s_ev.n

            @blk.tensor
            def _(t, hc=hc, wi=wi, ev_end=ev_end):
                t.wait_ge(s_ev.h, ev_end)
                if hc >= 1 or wi == 1:
                    t.wait_ge(s_hd.h, s_hd.n)
                for qg in range(2):
                    for kk in range(DC):
                        i = t.matmul(k.pp[2 * wi + qg][:],
                                     qT[wi][:, kk * 128:(kk + 1) * 128],
                                     k.xn[kk][:, qg * 512:(qg + 1) * 512],
                                     start=(kk == 0), stop=(kk == DC - 1))
                    pe.inc(i)

        pe_x12 = pe.n

        @blk.scalar
        def _(sc, hc=hc, pe_x12=pe_x12):
            sc.wait_ge(pe.h, pe_x12)
            for qg in range(2):
                ac.inc(sc.activation(sgt[:, qg * 512:(qg + 1) * 512],
                                     k.pp[qg][:], AF.Sigmoid))

        ac_sig = ac.n

        @blk.vector
        def _(ve, hc=hc, ac_sig=ac_sig, pe_x12=pe_x12):
            ve.wait_ge(ac.h, ac_sig)
            ve.wait_ge(pe.h, pe_x12)
            for qg in range(2):
                i = ve.tensor_mul(hidden[hc][:, qg * 512:(qg + 1) * 512],
                                  sgt[:, qg * 512:(qg + 1) * 512],
                                  k.pp[2 + qg][:])
                s_hd.inc(i)

    k.barrier()
    if stop_after == "hidden":
        hdbg = k.dram("hidden_dbg", [FFN_H, TOK], BF16, "ExternalOutput")

        @blk.sync
        def _(sy):
            for hc in range(HC):
                qs.inc(sy.dma_start(out=hdbg[hc * 128:(hc + 1) * 128, :],
                                    in_=hidden[hc][:]), 16)
        k.barrier()
        return None

    for dc in range(DC):
        @blk.sync
        def _(sy, dc=dc):
            if dc >= 1:
                sy.wait_ge(s_q.h, s_q.n)
            for wci, (c0, c1) in enumerate(w3cols):
                qs.inc(sy.dma_start(out=wstg[wci][:, 0:c1 - c0],
                                    in_=w3[dc * 128:(dc + 1) * 128, c0:c1]), 16)
            ld_w[(dc, 3)] = qs.n

        @blk.vector
        def _(ve, dc=dc):
            ve.wait_ge(qs.h, ld_w[(dc, 3)])
            if dc >= 1:
                ve.wait_ge(s_ev.h, s_ev.n)
            for wci, (c0, c1) in enumerate(w3cols):
                w = c1 - c0
                i = ve.tensor_scalar(qa[:, 0:w], wstg[wci][:, 0:w],
                                     ntau[2][:], None, ALU.is_lt)
                dv.inc(i)
                ve.wait_ge(dv.h, dv.n)
                i = ve.scalar_tensor_tensor(
                    qbf3[:, c0:c1], wstg[wci][:, 0:w], tau[2][:],
                    qa[:, 0:w], ALU.is_gt, ALU.subtract)
                s_q.inc(i)

        sq_now = s_q.n
        tr0 = s_tr.n
        ev0 = s_ev.n

        @blk.tensor
        def _(t, dc=dc, sq_now=sq_now, tr0=tr0, ev0=ev0):
            t.wait_ge(s_q.h, sq_now)
            for grp in range(11):
                t.wait_ge(s_ev.h, ev0 + grp)
                for s4 in range(4):
                    hc = grp * 4 + s4
                    i = t.transpose(k.pt_bf[:, s4 * 128:(s4 + 1) * 128],
                                    qbf3[:, hc * 128:(hc + 1) * 128],
                                    k.ident[:])
                s_tr.inc(i)

        @blk.scalar
        def _(sc, dc=dc, tr0=tr0):
            for grp in range(11):
                sc.wait_ge(s_tr.h, tr0 + grp + 1)
                i = sc.copy(q3T[:, grp * 512:(grp + 1) * 512], k.pt_bf[:])
                s_ev.inc(i)

        ev_end = s_ev.n

        @blk.tensor
        def _(t, dc=dc, ev_end=ev_end):
            t.wait_ge(s_ev.h, ev_end)
            if dc >= 1:
                t.wait_ge(s_gf.h, s_gf.n)
            for qg in range(2):
                for hc in range(HC):
                    i = t.matmul(k.pp[qg][:],
                                 q3T[:, hc * 128:(hc + 1) * 128],
                                 hidden[hc][:, qg * 512:(qg + 1) * 512],
                                 start=(hc == 0), stop=(hc == HC - 1))
                pe.inc(i)

        pe_f = pe.n

        @blk.scalar
        def _(sc, dc=dc, pe_f=pe_f):
            sc.wait_ge(pe.h, pe_f)
            for qg in range(2):
                i = sc.activation(gfm[dc][:, qg * 512:(qg + 1) * 512],
                                  k.pp[qg][:], AF.Silu)
                s_gf.inc(i)

    k.barrier()

    cmb = {}
    for dc in range(DC):
        @blk.sync
        def _(sy, dc=dc):
            if dc >= 2:
                sy.wait_ge(dv.h, cmb[dc - 2])
            qs.inc(sy.dma_start(out=xstg[dc % 2][:],
                                in_=x_ownT[dc * 128:(dc + 1) * 128, :]), 16)
            if standalone:
                qs.inc(sy.dma_start(out=sstg[dc % 2][:],
                                    in_=satt_in[dc * 128:(dc + 1) * 128, :]), 16)
            else:
                qs.inc(sy.dma_start(out=sstg[dc % 2][:],
                                    in_=satt_in[dc * 128:(dc + 1) * 128, :]), 16)
            ld_w[(dc, 9)] = qs.n

        qs_c = qs.n
        dma_c = {}

        @blk.vector
        def _(ve, dc=dc, qs_c=qs_c):
            ve.wait_ge(qs.h, qs_c)
            i = ve.tensor_mul(sgt[:], sstg[dc % 2][:], gfm[dc][:])
            dv.inc(i)
            ve.wait_ge(dv.h, dv.n)
            i = ve.tensor_add(xstg[dc % 2][:], xstg[dc % 2][:], sgt[:])
            dv.inc(i)
            cmb[dc] = dv.n

        @blk.sync
        def _(sy, dc=dc):
            sy.wait_ge(dv.h, cmb[dc])
            qs.inc(sy.dma_start(out=x1_out[dc * 128:(dc + 1) * 128, :],
                                in_=xstg[dc % 2][:]), 16)

    k.barrier()
    return None


def emit_l1_scan_chunks(k: K, whg, summ_pass, summ_out=None, hin=None,
                        hbf=None, sfx="", at=None):
    nc, blk = k.nc, k.block
    qs, qp, pe, dv, ac, po = k.qs, k.qp, k.pe, k.dv, k.ac, k.po
    s_wl = Sem(nc)

    if at is None:
        alloc = lambda n, s, d: k.sbuf(n, s, d)
    else:
        off = [at]

        def alloc(n, s, d, off=off):
            nbytes = (2 if d == BF16 else 4) * s[1]
            t = k.sbuf_at(n, s, d, off[0])
            off[0] += nbytes
            return t
    wstg = [alloc(f"l1w{sfx}{i}", [128, 4096], BF16) for i in range(2)]
    Abuf = alloc(f"l1A{sfx}", [128, TOK], F32)
    Vbuf = alloc(f"l1V{sfx}", [128, TOK], F32)
    sgA = alloc(f"l1s{sfx}", [128, TOK], F32)
    gtb = alloc(f"l1g{sfx}", [128, TOK], F32)
    mby = alloc(f"l1m{sfx}", [128, TOK], F32)
    zeros = alloc(f"l1z{sfx}", [128, 128], F32)
    Hloc = alloc(f"l1H{sfx}", [128, TOK], F32)
    Ploc = k.sbuf(f"l1P{sfx}", [128, TOK], F32) if summ_pass else None
    summ = k.sbuf(f"l1sm{sfx}", [128, 16], F32) if summ_pass else None

    @blk.gpsimd
    def _(g):
        g.memset(zeros[:], 0.0)
        po.inc(g.nop())

    k.barrier()

    ldw = {}
    hg_mm = {}
    for c in range(EC):
        @blk.gpsimd
        def _(g, c=c):
            if c >= 2:
                g.wait_ge(pe.h, hg_mm[c - 2])
            for dc in range(DC):
                src = bass.AP(tensor=whg.tensor,
                              offset=dc * 128 * 2 * DI + c * 128,
                              ap=[[2 * DI, 128], [EC * 128, 2], [1, 128]])
                qp.inc(g.dma_start(
                    out=wstg[c % 2][:, dc * 256:(dc + 1) * 256]
                    .rearrange("p (s w) -> p s w", s=2),
                    in_=src), 16)
            ldw[c] = qp.n

        @blk.tensor
        def _(t, c=c):
            t.wait_ge(qp.h, ldw[c])
            if c >= 1:
                t.wait_ge(dv.h, av_done[c - 1])
            for qg in range(2):
                for dc in range(DC):
                    i = t.matmul(k.pp[qg][:],
                                 wstg[c % 2][:, dc * 256:dc * 256 + 128],
                                 k.xn2[dc][:, qg * 512:(qg + 1) * 512],
                                 start=(dc == 0), stop=(dc == DC - 1))
                pe.inc(i)
                for dc in range(DC):
                    i = t.matmul(k.pp[2 + qg][:],
                                 wstg[c % 2][:, dc * 256 + 128:dc * 256 + 256],
                                 k.xn2[dc][:, qg * 512:(qg + 1) * 512],
                                 start=(dc == 0), stop=(dc == DC - 1))
                pe.inc(i)

        pe_hg = pe.n
        hg_mm[c] = pe.n
        if c == 0:
            av_done = {}
            ac_av = {}

        @blk.scalar
        def _(sc, c=c, pe_hg=pe_hg):
            for qg in range(2):
                sc.wait_ge(pe.h, pe_hg - 4 + 2 * qg + 2)
                if c >= 1:
                    sc.wait_ge(dv.h, av_done[c - 1])
                sl = slice(qg * 512, (qg + 1) * 512)
                ac.inc(sc.activation(Abuf[:, sl], k.pp[2 + qg][:], AF.Sigmoid,
                                     scale=-1.0))
                ac.inc(sc.activation(sgA[:, sl], k.pp[qg][:], AF.Sigmoid))
            ac_av[c] = ac.n

        @blk.vector
        def _(ve, c=c, pe_hg=pe_hg):
            for qg in range(2):
                sl = slice(qg * 512, (qg + 1) * 512)
                ve.wait_ge(ac.h, ac_av[c] - 2 * (1 - qg))
                ve.wait_ge(pe.h, pe_hg)
                ph = k.pp[qg]
                dv.inc(ve.tensor_single_scalar(mby[:, sl], ph[:], 0.0, ALU.is_ge))
                dv.inc(ve.tensor_single_scalar(gtb[:, sl], ph[:], 0.5, ALU.add))
                ve.wait_ge(dv.h, dv.n)
                dv.inc(ve.tensor_sub(gtb[:, sl], gtb[:, sl], sgA[:, sl]))
                ve.wait_ge(dv.h, dv.n)
                dv.inc(ve.tensor_mul(mby[:, sl], mby[:, sl], gtb[:, sl]))
                ve.wait_ge(dv.h, dv.n)
                dv.inc(ve.tensor_add(gtb[:, sl], sgA[:, sl], mby[:, sl]))
                ve.wait_ge(dv.h, dv.n)
                dv.inc(ve.tensor_scalar(mby[:, sl], Abuf[:, sl], -1.0, 1.0,
                                        ALU.mult, ALU.add))
                ve.wait_ge(dv.h, dv.n)
                dv.inc(ve.tensor_mul(Vbuf[:, sl], mby[:, sl], gtb[:, sl]))
                ve.wait_ge(dv.h, dv.n)
            av_done[c] = dv.n

        @blk.vector
        def _(ve, c=c):
            ve.wait_ge(dv.h, av_done[c])
            for j in range(NT):
                sl = slice(j * 128, (j + 1) * 128)
                init = 0.0 if summ_pass else hin[c][:, j:j + 1]
                dv.inc(ve.tensor_tensor_scan(Hloc[:, sl], Abuf[:, sl],
                                             Vbuf[:, sl], init,
                                             ALU.mult, ALU.add))
                if summ_pass:
                    dv.inc(ve.tensor_tensor_scan(Ploc[:, sl], Abuf[:, sl],
                                                 zeros[:], 1.0,
                                                 ALU.mult, ALU.add))
            ve.wait_ge(dv.h, dv.n)
            if summ_pass:
                for j in range(NT):
                    dv.inc(ve.tensor_copy(summ[:, 8 + j:9 + j],
                                          Hloc[:, j * 128 + 127:j * 128 + 128]))
                    dv.inc(ve.tensor_copy(summ[:, j:j + 1],
                                          Ploc[:, j * 128 + 127:j * 128 + 128]))
            else:
                dv.inc(ve.tensor_copy(hbf[c][:], Hloc[:]))
            ve.wait_ge(dv.h, dv.n)
        dv_done = dv.n

        if summ_pass:
            @blk.sync
            def _(sy, c=c, dv_done=dv_done):
                sy.wait_ge(dv.h, dv_done)
                qs.inc(sy.dma_start(out=summ_out[c * 128:(c + 1) * 128, :],
                                    in_=summ[:]), 16)
        if summ_pass:
            qs_s = qs.n

            @blk.vector
            def _(ve, qs_s=qs_s):
                ve.wait_ge(qs.h, qs_s)
    return None


def emit_l1a(k: K, standalone=True, stop_after=None):
    nc, blk = k.nc, k.block
    qs, qp, pe, dv, ac, po = k.qs, k.qp, k.pe, k.dv, k.ac, k.po

    x1_tm = k.dram("x1_tm", [TOK, D], F32, "ExternalInput")
    whg = k.dram("whg", [D, 2 * DI], F32, "ExternalInput")
    xn2_out = k.dram("xn2_out", [D, TOK], BF16, "ExternalOutput")
    summ_out = k.dram("summ_out", [DI, 16], F32, "ExternalOutput")

    if standalone:
        k.emit_identity()

    k.xn2 = [k.sbuf(f"xn2_{kk}", [128, TOK], BF16) for kk in range(DC)]
    xf = [k.sbuf(f"l1_xf{i}", [128, D], F32) for i in range(2)]
    xn_tm = k.sbuf("l1_xn_tm", [128, D], BF16)
    mv6 = k.sbuf("l1_mv6", [128, 24], F32)
    mv2 = k.sbuf("l1_mv2", [128, 2], F32)
    stat = k.sbuf("l1_stat", [128, 2], F32)
    xn_done = {}
    xnT_ev = {}

    for j in range(NT):
        @blk.sync
        def _(sy, j=j):
            if j >= 2:
                sy.wait_ge(dv.h, xn_done[j - 2])
            qs.inc(sy.dma_start(out=xf[j % 2][:],
                                in_=x1_tm[j * 128:(j + 1) * 128, :]), 16)

        qs_l = qs.n

        @blk.vector
        def _(ve, j=j, qs_l=qs_l):
            ve.wait_ge(qs.h, qs_l)
            for gq in range(4):
                dv.inc(ve.bn_stats(mv6[:, 6 * gq:6 * gq + 6],
                                   xf[j % 2][:, 512 * gq:512 * (gq + 1)]))
            ve.wait_ge(dv.h, dv.n)
            dv.inc(ve.bn_aggr(mv2[:], mv6[:]))
            ve.wait_ge(dv.h, dv.n)
            dv.inc(ve.tensor_scalar_add(stat[:, 1:2], mv2[:, 1:2], LN_EPS))
            dv.inc(ve.tensor_copy(stat[:, 0:1], mv2[:, 0:1]))
            ve.wait_ge(dv.h, dv.n)
            ve.wait_ge(ac.h, ac.n + 1)
            dv.inc(ve.reciprocal(stat[:, 1:2], stat[:, 1:2]))
            ve.wait_ge(dv.h, dv.n)
            if j >= 1:
                ve.wait_ge(ac.h, xnT_ev[j - 1])
            dv.inc(ve.tensor_scalar(xn_tm[:], xf[j % 2][:], stat[:, 0:1],
                                    stat[:, 1:2], ALU.subtract, ALU.mult))
            xn_done[j] = dv.n

        @blk.scalar
        def _(sc, j=j):
            sc.wait_ge(dv.h, dv.n - 2)
            ac.inc(sc.activation(stat[:, 1:2], stat[:, 1:2], AF.Sqrt))

        dv_xn = dv.n
        pe_t0, ac_t0 = pe.n, ac.n

        @blk.tensor
        def _(t, j=j, dv_xn=dv_xn, ac_t0=ac_t0):
            t.wait_ge(dv.h, dv_xn)
            for grp in range(4):
                if grp >= 1:
                    t.wait_ge(ac.h, ac_t0 + grp)
                for s4 in range(4):
                    kk = grp * 4 + s4
                    i = t.transpose(k.pt_bf[:, s4 * 128:(s4 + 1) * 128],
                                    xn_tm[:, kk * 128:(kk + 1) * 128], k.ident[:])
                pe.inc(i)

        @blk.scalar
        def _(sc, j=j, pe_t0=pe_t0):
            for grp in range(4):
                sc.wait_ge(pe.h, pe_t0 + grp + 1)
                for s4 in range(4):
                    kk = grp * 4 + s4
                    i = sc.copy(k.xn2[kk][:, j * 128:(j + 1) * 128],
                                k.pt_bf[:, s4 * 128:(s4 + 1) * 128])
                ac.inc(i)
            xnT_ev[j] = ac.n

    k.barrier()

    @blk.sync
    def _(sy):
        for kk in range(DC):
            qs.inc(sy.dma_start(out=xn2_out[kk * 128:(kk + 1) * 128, :],
                                in_=k.xn2[kk][:]), 16)

    k.barrier()
    if stop_after == "xn2":
        return None

    emit_l1_scan_chunks(k, whg, summ_pass=True, summ_out=summ_out, sfx="a")
    k.barrier()
    return None


def emit_l1b(k: K, standalone=True, stop_after=None):
    nc, blk = k.nc, k.block
    qs, qp, pe, dv, ac, po = k.qs, k.qp, k.pe, k.dv, k.ac, k.po

    xn2_in = k.dram("xn2_in", [D, TOK], BF16, "ExternalInput")
    x1_tm = k.dram("x1_tm", [TOK, D], F32, "ExternalInput")
    summ_all = k.dram("summ_all", [DI, 128], F32, "ExternalInput")
    hsel_in = k.dram("hsel_in", [128, 8 * 64], F32, "ExternalInput")
    whg = k.dram("whg", [D, 2 * DI], F32, "ExternalInput")
    w_out = k.dram("w_out", [DI, D], F32, "ExternalInput")
    wr_in = k.dram("wr_in", [D, 8], F32, "ExternalInput")
    we_in = k.dram("we_in", [8, D, D], F32, "ExternalInput")
    bebc_in = k.dram("bebc_in", [8, 128, D], F32, "ExternalInput")
    y_out = k.dram("y_out", [TOK, D], F32, "ExternalOutput")
    s_dram = k.dram("s_scratch", [NT, 128, D], BF16, "Internal")

    if standalone:
        k.emit_identity()
    k.xn2 = [k.sbuf(f"xn2_{kk}", [128, TOK], BF16) for kk in range(DC)]

    @blk.sync
    def _(sy):
        for kk in range(DC):
            qs.inc(sy.dma_start(out=k.xn2[kk][:],
                                in_=xn2_in[kk * 128:(kk + 1) * 128, :]), 16)

    sust = k.sbuf("l1b_sust", [128, 128], F32)
    hsh = k.sbuf("l1b_hsh", [128, 64], F32)
    hsel = k.sbuf("l1b_hsel", [128, 8 * 64], F32)
    hdot = k.sbuf("l1b_hdot", [128, 64], F32)
    hin = [k.sbuf(f"l1b_hin{c}", [128, NT], F32) for c in range(EC)]

    @blk.sync
    def _(sy):
        qs.inc(sy.dma_start(out=hsel[:], in_=hsel_in), 16)

    k.barrier()
    sudone = {}
    for c in range(EC):
        @blk.sync
        def _(sy, c=c):
            if c >= 1:
                sy.wait_ge(dv.h, sudone[c - 1])
            qs.inc(sy.dma_start(out=sust[:],
                                in_=summ_all[c * 128:(c + 1) * 128, :]), 16)

        qs_su = qs.n

        @blk.vector
        def _(ve, c=c, qs_su=qs_su):
            ve.wait_ge(qs.h, qs_su)
            dv.inc(ve.tensor_tensor_scan(hsh[:], sust[:, 0:64],
                                         sust[:, 64:128], 0.0,
                                         ALU.mult, ALU.add))
            ve.wait_ge(dv.h, dv.n)
            dv.inc(ve.tensor_copy(sust[:, 1:64], hsh[:, 0:63]))
            ve.wait_ge(dv.h, dv.n)
            dv.inc(ve.memset(sust[:, 0:1], 0.0))
            ve.wait_ge(dv.h, dv.n)
            for j in range(NT):
                dv.inc(ve.tensor_mul(hdot[:], sust[:, 0:64],
                                     hsel[:, j * 64:(j + 1) * 64]))
                ve.wait_ge(dv.h, dv.n)
                dv.inc(ve.reduce_sum(hin[c][:, j:j + 1], hdot[:], axis=AX.X))
                ve.wait_ge(dv.h, dv.n)
            sudone[c] = dv.n

    k.barrier()
    if stop_after == "hin":
        hdbg = k.dram("hin_dbg", [EC, 128, NT], F32, "ExternalOutput")

        @blk.sync
        def _(sy):
            for c in range(EC):
                qs.inc(sy.dma_start(out=hdbg[c], in_=hin[c][:]), 16)
        k.barrier()
        return None

    hbf = [k.sbuf(f"l1b_hbf{c}", [128, TOK], BF16) for c in range(EC)]
    k._l1b_ar = k.arena("l1b_arena", 98304)
    emit_l1_scan_chunks(k, whg, summ_pass=False, hin=hin, hbf=hbf, sfx="b",
                        at=k._l1b_ar)
    k.barrier()
    if stop_after == "H":
        hdbg2 = k.dram("H_dbg", [DI, TOK], BF16, "ExternalOutput")

        @blk.sync
        def _(sy):
            for c in range(EC):
                qs.inc(sy.dma_start(out=hdbg2[c * 128:(c + 1) * 128, :],
                                    in_=hbf[c][:]), 16)
        k.barrier()
        return None

    wout_res = [k.sbuf_at(f"l1b_wo{ec}", [128, D], BF16,
                          k._l1b_ar + 4096 * ec) for ec in range(EC)]
    s_stg = [k.sbuf(f"l1b_sst{i}", [128, D], BF16) for i in range(2)]

    @blk.gpsimd
    def _(g):
        for ec in range(EC):
            qp.inc(g.dma_start(out=wout_res[ec][:],
                               in_=w_out[ec * 128:(ec + 1) * 128, :]), 16)

    k.barrier()
    sdma = {}
    for qt in range(NT):
        @blk.tensor
        def _(t, qt=qt):
            if qt >= 1:
                t.wait_ge(ac.h, sev[qt - 1])
            for grp in range(4):
                for ec in range(EC):
                    i = t.matmul(k.pp[grp][:],
                                 hbf[ec][:, qt * 128:(qt + 1) * 128],
                                 wout_res[ec][:, grp * 512:(grp + 1) * 512],
                                 start=(ec == 0), stop=(ec == EC - 1))
                pe.inc(i)

        pe_s = pe.n
        if qt == 0:
            sev = {}

        @blk.scalar
        def _(sc, qt=qt, pe_s=pe_s):
            sc.wait_ge(pe.h, pe_s)
            if qt >= 2:
                sc.wait_ge(qs.h, sdma[qt - 2])
            for grp in range(4):
                i = sc.copy(s_stg[qt % 2][:, grp * 512:(grp + 1) * 512],
                            k.pp[grp][:])
            ac.inc(i)
        sev[qt] = ac.n

        @blk.sync
        def _(sy, qt=qt, an=ac.n):
            sy.wait_ge(ac.h, an)
            qs.inc(sy.dma_start(out=s_dram[qt], in_=s_stg[qt % 2][:]), 16)
            sdma[qt] = qs.n

    k.barrier()

    wr_sb = [k.sbuf(f"l1b_wr{dc}", [128, 8], BF16) for dc in range(DC)]
    lg = k.sbuf("l1b_lg", [128, 8], F32)
    m1 = k.sbuf("l1b_m1", [128, 1], F32)
    m2 = k.sbuf("l1b_m2", [128, 1], F32)
    eq1 = k.sbuf("l1b_eq1", [128, 8], F32)
    eq2 = k.sbuf("l1b_eq2", [128, 8], F32)
    g1 = k.sbuf("l1b_g1", [128, 1], F32)
    gate_all = k.sbuf("l1b_gate", [128, 8 * NT], F32)
    inv128 = k.sbuf("l1b_inv128", [128, 128], BF16)
    bb_sb = k.sbuf("l1b_bb", [128, D], BF16)
    yacc = [k.sbuf_at(f"l1b_yacc{qt}", [128, D], BF16,
                      k._l1b_ar + 65536 + 4096 * qt) for qt in range(NT)]
    ygt = k.sbuf("l1b_ygt", [128, D], BF16)

    @blk.gpsimd
    def _(g):
        g.memset(inv128[:], 1.0 / 128.0)
        for dc in range(DC):
            qp.inc(g.dma_start(out=wr_sb[dc][:],
                               in_=wr_in[dc * 128:(dc + 1) * 128, :]), 16)
        for qt in range(NT - 1):
            g.memset(yacc[qt][:], 0.0)
        po.inc(g.memset(yacc[NT - 1][:], 0.0))

    k.barrier()
    for qt in range(NT):
        @blk.tensor
        def _(t, qt=qt):
            if qt >= 1:
                t.wait_ge(dv.h, gdone[qt - 1])
            for dc in range(DC):
                i = t.matmul(k.pt32[:, 0:8],
                             k.xn2[dc][:, qt * 128:(qt + 1) * 128],
                             wr_sb[dc][:], start=(dc == 0), stop=(dc == DC - 1))
            pe.inc(i)

        pe_lg = pe.n
        if qt == 0:
            gdone = {}

        @blk.vector
        def _(ve, qt=qt, pe_lg=pe_lg):
            ve.wait_ge(pe.h, pe_lg)
            dv.inc(ve.tensor_copy(lg[:], k.pt32[:, 0:8]))
            ve.wait_ge(dv.h, dv.n)
            dv.inc(ve.reduce_max(m1[:], lg[:], axis=AX.X))
            ve.wait_ge(dv.h, dv.n)
            dv.inc(ve.tensor_scalar(eq1[:], lg[:], m1[:], None, ALU.is_equal))
            ve.wait_ge(dv.h, dv.n)
            dv.inc(ve.scalar_tensor_tensor(eq2[:], eq1[:], NEG, lg[:],
                                           ALU.mult, ALU.add))
            ve.wait_ge(dv.h, dv.n)
            dv.inc(ve.reduce_max(m2[:], eq2[:], axis=AX.X))
            ve.wait_ge(dv.h, dv.n)
            dv.inc(ve.tensor_scalar(eq2[:], eq2[:], m2[:], None, ALU.is_equal))
            dv.inc(ve.tensor_sub(m1[:], m1[:], m2[:]))
            ve.wait_ge(dv.h, dv.n)
            ve.wait_ge(ac.h, ac.n + 1)
            dv.inc(ve.tensor_scalar(eq1[:], eq1[:], g1[:], None, ALU.mult))
            ve.wait_ge(dv.h, dv.n)
            dv.inc(ve.tensor_scalar(g1[:], g1[:], -1.0, 1.0, ALU.mult, ALU.add))
            ve.wait_ge(dv.h, dv.n)
            dv.inc(ve.tensor_scalar(eq2[:], eq2[:], g1[:], None, ALU.mult))
            ve.wait_ge(dv.h, dv.n)
            dv.inc(ve.tensor_add(gate_all[:, qt * 8:(qt + 1) * 8],
                                 eq1[:], eq2[:]))
            gdone[qt] = dv.n

        @blk.scalar
        def _(sc, qt=qt):
            sc.wait_ge(dv.h, dv.n - 4)
            ac.inc(sc.activation(g1[:], m1[:], AF.Sigmoid))

    k.barrier()
    if stop_after == "gates":
        gdbg = k.dram("gates_dbg", [128, 8 * NT], F32, "ExternalOutput")

        @blk.sync
        def _(sy):
            qs.inc(sy.dma_start(out=gdbg, in_=gate_all[:]), 16)
        k.barrier()
        return None

    we_res = [k.sbuf_at(f"l1b_we{dc}", [128, D], BF16,
                        k.nc.lookup_mloc(wout_res[dc]).addr) for dc in range(DC)]
    ldwe = {}
    emm = {}
    emmq = {}
    yev = {}
    s_ya = Sem(nc)
    for e in range(8):
        @blk.gpsimd
        def _(g, e=e):
            if e >= 1:
                g.wait_ge(pe.h, emm[e - 1])
            for dc in range(DC):
                qp.inc(g.dma_start(out=we_res[dc][:], in_=we_in[e][dc * 128:(dc + 1) * 128, :]), 16)
            qp.inc(g.dma_start(out=bb_sb[:], in_=bebc_in[e]), 16)
            ldwe[e] = qp.n

        @blk.tensor
        def _(t, e=e):
            t.wait_ge(qp.h, ldwe[e])
            for qt in range(NT):
                if e * NT + qt >= 1:
                    t.wait_ge(s_ya.h, e * NT + qt)
                for grp in range(4):
                    for dc in range(DC):
                        i = t.matmul(k.pp[grp + (qt % 2) * 0][:],
                                     k.xn2[dc][:, qt * 128:(qt + 1) * 128],
                                     we_res[dc][:, grp * 512:(grp + 1) * 512],
                                     start=(dc == 0), stop=False)
                    i = t.matmul(k.pp[grp][:], inv128[:],
                                 bb_sb[:, grp * 512:(grp + 1) * 512],
                                 start=False, stop=True)
                pe.inc(i)
                emmq[(e, qt)] = pe.n
        emm[e] = pe.n

        @blk.scalar
        def _(sc, e=e):
            for qt in range(NT):
                sc.wait_ge(pe.h, emmq[(e, qt)])
                if e * NT + qt >= 1:
                    sc.wait_ge(s_ya.h, e * NT + qt)
                for grp in range(4):
                    i = sc.activation(ygt[:, grp * 512:(grp + 1) * 512],
                                      k.pp[grp][:], AF.Copy,
                                      scale=gate_all[:, qt * 8 + e:qt * 8 + e + 1])
                ac.inc(i)
                yev[(e, qt)] = ac.n

        @blk.vector
        def _(ve, e=e):
            for qt in range(NT):
                ve.wait_ge(ac.h, yev[(e, qt)])
                i = ve.tensor_add(yacc[qt][:], yacc[qt][:], ygt[:])
                s_ya.inc(i)

    k.barrier()

    xstg2 = [k.sbuf_at(f"l1b_x{i}", [128, D], F32, k._l1b_ar + 8192 * i)
             for i in range(2)]
    tprod = k.sbuf_at("l1b_tp", [128, D], F32, k._l1b_ar + 16384)
    cmb = {}
    for qt in range(NT):
        @blk.sync
        def _(sy, qt=qt):
            if qt >= 2:
                sy.wait_ge(dv.h, cmb[qt - 2])
            qs.inc(sy.dma_start(out=xstg2[qt % 2][:],
                                in_=x1_tm[qt * 128:(qt + 1) * 128, :]), 16)
            qs.inc(sy.dma_start(out=s_stg[qt % 2][:], in_=s_dram[qt]), 16)

        qs_c = qs.n

        @blk.scalar
        def _(sc, qt=qt):
            ac.inc(sc.activation(yacc[qt][:], yacc[qt][:], AF.Silu))

        ac_c = ac.n

        @blk.vector
        def _(ve, qt=qt, qs_c=qs_c, ac_c=ac_c):
            ve.wait_ge(qs.h, qs_c)
            ve.wait_ge(ac.h, ac_c)
            dv.inc(ve.tensor_mul(tprod[:], s_stg[qt % 2][:], yacc[qt][:]))
            ve.wait_ge(dv.h, dv.n)
            dv.inc(ve.tensor_add(xstg2[qt % 2][:], xstg2[qt % 2][:], tprod[:]))
            cmb[qt] = dv.n

        @blk.sync
        def _(sy, qt=qt, dn=None):
            sy.wait_ge(dv.h, cmb[qt])
            qs.inc(sy.dma_start(out=y_out[qt * 128:(qt + 1) * 128, :],
                                in_=xstg2[qt % 2][:]), 16)

    k.barrier()
    return None


def build_pre(dbg=False, stop_after=None):
    k = K()
    emit_pre(k, dbg=dbg, stop_after=stop_after)
    nc = k.finish()
    return nc


_PROGRAMS = {}


def _get_program(name):
    if name in _PROGRAMS:
        return _PROGRAMS[name]
    k = K(pool="attn" if name == "attn" else "default")
    if name == "pre":
        emit_pre(k, dbg=True)
    elif name == "attn":
        emit_attn(k, standalone=True)
    elif name == "ffn":
        emit_ffn(k, standalone=True)
    elif name == "l1a":
        emit_l1a(k, standalone=True)
    elif name == "l1b":
        emit_l1b(k, standalone=True)
    nc = k.finish()
    _PROGRAMS[name] = nc
    return nc


def _own_rows(r):
    idx = []
    for j in range(NT):
        g = r + 8 * j
        idx.extend(range(g * 128, (g + 1) * 128))
    return np.array(idx)


def _core_mask(r):
    m = np.zeros((128, 1024), np.float32)
    qi = np.arange(128)[:, None]
    ki = np.arange(128)[None, :]
    for lt in range(8):
        if lt == r:
            m[:, lt * 128:(lt + 1) * 128] = np.where(ki <= qi, 0.0, NEG)
        elif lt > r:
            m[:, lt * 128:(lt + 1) * 128] = NEG
    return m


def _hsel(r):
    m = np.zeros((128, 8 * 64), np.float32)
    for j in range(NT):
        g = r + 8 * j
        m[:, j * 64 + g] = 1.0
    return m


def kernel(L0_attn_x, L0_attn_W_q, L0_attn_Wd_kv, L0_attn_W_uv, L0_attn_W_uk,
           L0_attn_W_o, L0_attn_gamma, L0_attn_beta, L0_ffn_w1, L0_ffn_w2,
           L0_ffn_w3, L1_attn_w_hg, L1_attn_w_out, L1_ffn_Wr, L1_ffn_We,
           L1_ffn_be):
    bf16 = ml_dtypes.bfloat16
    f32 = np.float32
    x = np.ascontiguousarray(np.asarray(L0_attn_x, f32)[0])
    rows = [_own_rows(r) for r in range(NCORE)]
    xo = [np.ascontiguousarray(x[rows[r]]) for r in range(NCORE)]
    xoT = [np.ascontiguousarray(xo[r].T) for r in range(NCORE)]

    nc = _get_program("pre")
    gbc = np.ascontiguousarray(np.broadcast_to(
        np.asarray(L0_attn_gamma, f32)[None, :], (128, LAT)))
    bbc = np.ascontiguousarray(np.broadcast_to(
        np.asarray(L0_attn_beta, f32)[None, :], (128, LAT)))
    in_maps = [{
        "x_full": x, "x_own": xo[r],
        "wdkv": np.asarray(L0_attn_Wd_kv, f32),
        "wuv": np.asarray(L0_attn_W_uv, f32),
        "gamma_bc": gbc, "beta_bc": bbc,
    } for r in range(NCORE)]
    res = run_bass_kernel_spmd(nc, in_maps, core_ids=list(range(NCORE)))
    v_full = res.results[0]["v_dram"]
    ckv = res.results[0]["ckv_out"]
    xn = [res.results[r]["xn_out"] for r in range(NCORE)]

    nc = _get_program("attn")
    in_maps = [{
        "wq": np.asarray(L0_attn_W_q, f32),
        "wuk": np.asarray(L0_attn_W_uk, f32),
        "wo": np.asarray(L0_attn_W_o, f32),
        "maskin": _core_mask(r),
        "v_dram": v_full, "xn_in": xn[r], "ckv_in": ckv,
    } for r in range(NCORE)]
    res = run_bass_kernel_spmd(nc, in_maps, core_ids=list(range(NCORE)))
    satt = [res.results[r]["satt_out"] for r in range(NCORE)]

    nc = _get_program("ffn")
    in_maps = [{
        "w1": np.asarray(L0_ffn_w1, f32), "w2": np.asarray(L0_ffn_w2, f32),
        "w3": np.asarray(L0_ffn_w3, f32),
        "x_ownT": xoT[r], "xn_in": xn[r], "satt_in": satt[r],
    } for r in range(NCORE)]
    res = run_bass_kernel_spmd(nc, in_maps, core_ids=list(range(NCORE)))
    x1T = [res.results[r]["x1_out"] for r in range(NCORE)]
    x1tm = [np.ascontiguousarray(x1T[r].T) for r in range(NCORE)]

    nc = _get_program("l1a")
    whg = np.asarray(L1_attn_w_hg, f32)
    in_maps = [{"x1_tm": x1tm[r], "whg": whg} for r in range(NCORE)]
    res = run_bass_kernel_spmd(nc, in_maps, core_ids=list(range(NCORE)))
    xn2 = [res.results[r]["xn2_out"] for r in range(NCORE)]
    summ_all = np.empty((DI, 128), np.float32)
    for r in range(NCORE):
        su = res.results[r]["summ_out"]
        for j in range(NT):
            g = r + 8 * j
            summ_all[:, g] = su[:, j]
            summ_all[:, 64 + g] = su[:, 8 + j]

    nc = _get_program("l1b")
    bebc = np.ascontiguousarray(np.broadcast_to(
        np.asarray(L1_ffn_be, f32)[:, None, :], (8, 128, D)))
    in_maps = [{
        "xn2_in": xn2[r], "x1_tm": x1tm[r], "summ_all": summ_all,
        "hsel_in": _hsel(r), "whg": whg,
        "w_out": np.asarray(L1_attn_w_out, f32),
        "wr_in": np.asarray(L1_ffn_Wr, f32),
        "we_in": np.asarray(L1_ffn_We, f32),
        "bebc_in": bebc,
    } for r in range(NCORE)]
    res = run_bass_kernel_spmd(nc, in_maps, core_ids=list(range(NCORE)))

    y = np.empty((S, D), np.float32)
    for r in range(NCORE):
        y[rows[r]] = res.results[r]["y_out"]
    return y.reshape(1, S, D)


# version 07
# speedup vs baseline: 1.2707x; 1.2707x over previous
import numpy as np
import ml_dtypes
import concourse.bass as bass
import concourse.mybir as mybir
from concourse.bass_utils import run_bass_kernel_spmd

F32 = mybir.dt.float32
BF16 = mybir.dt.bfloat16
AF = mybir.ActivationFunctionType
ALU = mybir.AluOpType
AX = mybir.AxisListType

NCORE = 8
S = 8192
D = 2048
NH = 16
DH = 128
LAT = 512
TOK = S // NCORE
NT = TOK // 128
GT = S // 128
DC = D // 128
LC = LAT // 128
FFN_H = 5632
HC = FFN_H // 128
DI = 3072
EC = DI // 128
LN_EPS = 1e-5
NEG = -1.0e30
QSCALE = float(1.0 / np.sqrt(128.0))


class Sem:
    __slots__ = ("h", "n")

    def __init__(self, nc):
        self.h = nc.alloc_semaphore()
        self.n = 0

    def inc(self, instr, amt=1):
        instr.then_inc(self.h, amt)
        self.n += amt
        return self.n

    def wait(self, eng, val=None):
        eng.wait_ge(self.h, self.n if val is None else val)


class K:

    def __init__(self, pool="default"):
        nc = bass.Bass("TRN2", target_bir_lowering=False, debug=False)
        self.nc = nc
        self.qs = Sem(nc)
        self.qp = Sem(nc)
        self.pe = Sem(nc)
        self.dv = Sem(nc)
        self.ac = Sem(nc)
        self.po = Sem(nc)
        self.bar = Sem(nc)
        if pool == "attn":
            self.pp = [nc.alloc_psum_tensor(f"pp{i}", [128, 512], F32)
                       for i in range(6)]
            self.pt_bf = nc.alloc_psum_tensor("pt_bf", [128, 512], BF16)
            self.pt_bf2 = nc.alloc_psum_tensor("pt_bf2", [128, 512], BF16)
            self.pt32 = None
        elif pool == "ffn":
            self.pp = [nc.alloc_psum_tensor(f"pp{i}", [128, 512], F32)
                       for i in range(4)]
            self.pt_bf = nc.alloc_psum_tensor("pt_bf", [128, 512], BF16)
            self.pt_bf2 = nc.alloc_psum_tensor("pt_bf2", [128, 512], BF16)
            self.pt32 = nc.alloc_psum_tensor("pt32", [128, 128], F32)
        else:
            self.pp = [nc.alloc_psum_tensor(f"pp{i}", [128, 512], F32)
                       for i in range(6)]
            self.pt_bf = nc.alloc_psum_tensor("pt_bf", [128, 512], BF16)
            self.pt32 = nc.alloc_psum_tensor("pt32", [128, 128], F32)
        self.dummy_dv = nc.alloc_sbuf_tensor("dummy_dv", [1, 1], F32)
        self.dummy_ac = nc.alloc_sbuf_tensor("dummy_ac", [1, 1], F32)
        self.block_guard = nc.Block()
        self.block = self.block_guard.__enter__()

    def sbuf(self, name, shape, dtype):
        return self.nc.alloc_sbuf_tensor(name, shape, dtype)

    def arena(self, name, bytes_per_part):
        h = self.nc.alloc_sbuf_tensor(name, [128, bytes_per_part // 2], BF16)
        return self.nc.lookup_mloc(h).addr

    def sbuf_at(self, name, shape, dtype, offset):
        return self.nc.alloc_sbuf_tensor_at(name, shape, dtype, offset=offset)

    def dram(self, name, shape, dtype, kind="Internal"):
        return self.nc.dram_tensor(name, shape, dtype, kind=kind).ap()

    def finish(self):
        qs, qp = self.qs, self.qp
        blk = self.block

        @blk.sync
        def _(sy):
            qs.wait(sy)

        @blk.gpsimd
        def _(g):
            qp.wait(g)

        self.block_guard.__exit__(None, None, None)
        return self.nc

    def barrier(self):
        blk, bar = self.block, self.bar
        target = bar.n + 5
        qs, qp = self.qs, self.qp

        @blk.sync
        def _(sy):
            qs.wait(sy)
            sy.nop().then_inc(bar.h, 1)
            sy.wait_ge(bar.h, target)

        @blk.gpsimd
        def _(g):
            qp.wait(g)
            g.nop().then_inc(bar.h, 1)
            g.wait_ge(bar.h, target)

        @blk.vector
        def _(ve):
            ve.memset(self.dummy_dv[:], 0.0).then_inc(bar.h, 1)
            ve.wait_ge(bar.h, target)

        @blk.scalar
        def _(sc):
            sc.memzero(self.dummy_ac[:]).then_inc(bar.h, 1)
            sc.wait_ge(bar.h, target)

        @blk.tensor
        def _(t):
            t.nop().then_inc(bar.h, 1)
            t.wait_ge(bar.h, target)

        bar.n = target

    def emit_identity(self):
        self.ident = self.sbuf("ident_bf", [128, 128], BF16)
        self.ident32 = self.sbuf("ident_f32", [128, 128], F32)
        po = self.po

        @self.block.gpsimd
        def _(g):
            g.memset(self.ident[:], 1.0)
            po.inc(g.affine_select(self.ident[:], self.ident[:], [[-1, 128]],
                                   ALU.is_equal, 0.0, base=0,
                                   channel_multiplier=1))
            g.wait_ge(po.h, po.n)
            g.memset(self.ident32[:], 1.0)
            po.inc(g.affine_select(self.ident32[:], self.ident32[:], [[-1, 128]],
                                   ALU.is_equal, 0.0, base=0,
                                   channel_multiplier=1))

    def transpose_serial(self, pairs, dtype=BF16, width=128):
        blk, pe, ac = self.block, self.pe, self.ac
        pt = self.pt_bf if dtype == BF16 else self.pt32
        ident = self.ident if dtype == BF16 else self.ident32
        pe0, ac0 = pe.n, ac.n
        nslot = 4 if (dtype == BF16 and width <= 128) else 1
        groups = [pairs[i:i + nslot] for i in range(0, len(pairs), nslot)]

        @blk.tensor
        def _(t):
            for gi, grp in enumerate(groups):
                if gi >= 1:
                    t.wait_ge(ac.h, ac0 + gi)
                for si, (src, _dst) in enumerate(grp):
                    slot = si * width
                    p = src.shape[0]
                    i = t.transpose(pt[0:src.shape[-1], slot:slot + width],
                                    src, ident[0:p, 0:p])
                pe.inc(i)

        @blk.scalar
        def _(sc):
            for gi, grp in enumerate(groups):
                sc.wait_ge(pe.h, pe0 + gi + 1)
                for si, (_src, dst) in enumerate(grp):
                    slot = si * width
                    i = sc.copy(dst, pt[0:dst.shape[0], slot:slot + width])
                ac.inc(i)

    def bcast_rows(self, jobs):
        blk, pe, ac = self.block, self.pe, self.ac
        pb = self.pp[5]
        pe0, ac0 = pe.n, ac.n

        @blk.tensor
        def _(t):
            for i, (row, _dst) in enumerate(jobs):
                if i >= 1:
                    t.wait_ge(ac.h, ac0 + i)
                pe.inc(t.matmul(pb[:, 0:row.shape[-1]], self.ones_row32[:],
                                row, start=True, stop=True))

        @blk.scalar
        def _(sc):
            for i, (row, dst) in enumerate(jobs):
                sc.wait_ge(pe.h, pe0 + i + 1)
                ac.inc(sc.copy(dst, pb[:, 0:row.shape[-1]]))




def emit_pre(k: K, dbg=False, stop_after=None):
    nc, blk = k.nc, k.block
    x_full = k.dram("x_full", [S, D], F32, "ExternalInput")
    x_own = k.dram("x_own", [TOK, D], F32, "ExternalInput")
    wdkv = k.dram("wdkv", [LAT, D], F32, "ExternalInput")
    wuv = k.dram("wuv", [D, LAT], F32, "ExternalInput")
    gamma_bc = k.dram("gamma_bc", [128, LAT], F32, "ExternalInput")
    beta_bc = k.dram("beta_bc", [128, LAT], F32, "ExternalInput")

    okind = "ExternalOutput"
    v_dram = k.dram("v_dram", [GT, 128, D], BF16, okind if dbg else "Internal")
    ckv_out = k.dram("ckv_out", [LAT, S], BF16, okind)
    xn_out = k.dram("xn_out", [D, TOK], BF16, okind)

    qs, qp, pe, dv, ac, po = k.qs, k.qp, k.pe, k.dv, k.ac, k.po

    k.emit_identity()
    wnat = [k.sbuf(f"wnat{c}", [128, D], BF16) for c in range(LC)]
    wdkvT = [k.sbuf(f"wdkvT{kk}", [128, LAT], BF16) for kk in range(DC)]
    wuvT = [k.sbuf(f"wuvT{c}", [128, D], BF16) for c in range(LC)]
    gbc = k.sbuf("gbc", [128, LAT], F32)
    bbc = k.sbuf("bbc", [128, LAT], F32)

    @blk.gpsimd
    def _(g):
        for c in range(LC):
            qp.inc(g.dma_start(out=wnat[c][:],
                               in_=wdkv[c * 128:(c + 1) * 128, :]), 16)

    @blk.sync
    def _(sy):
        qs.inc(sy.dma_start(out=gbc[:], in_=gamma_bc), 16)
        qs.inc(sy.dma_start(out=bbc[:], in_=beta_bc), 16)

    k.barrier()
    k.transpose_serial(
        [(wnat[c][:, kk * 128:(kk + 1) * 128],
          wdkvT[kk][:, c * 128:(c + 1) * 128])
         for c in range(LC) for kk in range(DC)])
    k.barrier()

    @blk.gpsimd
    def _(g):
        for kk in range(DC):
            qp.inc(g.dma_start(
                out=wnat[kk // 4][:, (kk % 4) * 512:(kk % 4) * 512 + 512],
                in_=wuv[kk * 128:(kk + 1) * 128, :]), 16)

    k.barrier()
    k.transpose_serial(
        [(wnat[kk // 4][:, (kk % 4) * 512 + c * 128:
                        (kk % 4) * 512 + (c + 1) * 128],
          wuvT[c][:, kk * 128:(kk + 1) * 128])
         for kk in range(DC) for c in range(LC)])
    k.barrier()
    if stop_after == "consts":
        return None

    k.ckv = [k.sbuf(f"ckv{c}", [128, S], BF16) for c in range(LC)]
    xf = [k.sbuf(f"xf{i}", [128, D], F32) for i in range(2)]
    xn_tm = k.sbuf("xn_tm", [128, D], BF16)
    xnT = [k.sbuf(f"xnT{i}", [128, D], BF16) for i in range(2)]
    mv6 = k.sbuf("mv6", [128, 24], F32)
    mv2 = k.sbuf("mv2", [128, 2], F32)
    stat = k.sbuf("stat", [128, 2], F32)
    cstat = k.sbuf("cstat", [128, 2], F32)
    ctm = k.sbuf("ctm", [128, LAT], BF16)
    v_sb = [k.sbuf(f"v_sb{i}", [128, D], BF16) for i in range(2)]

    def tile_stats(ve, src_ap, dst_stat):
        for gq in range(4):
            dv.inc(ve.bn_stats(mv6[:, 6 * gq:6 * gq + 6],
                               src_ap[:, 512 * gq:512 * (gq + 1)]))
        ve.wait_ge(dv.h, dv.n)
        dv.inc(ve.bn_aggr(mv2[:], mv6[:]))
        ve.wait_ge(dv.h, dv.n)
        dv.inc(ve.tensor_scalar_add(dst_stat[:, 1:2], mv2[:, 1:2], LN_EPS))
        dv.inc(ve.tensor_copy(dst_stat[:, 0:1], mv2[:, 0:1]))
        ve.wait_ge(dv.h, dv.n)
        ve.wait_ge(ac.h, ac.n + 1)
        dv.inc(ve.reciprocal(dst_stat[:, 1:2], dst_stat[:, 1:2]))

    def tile_stats_sqrt(sc, dst_stat, dv_before):
        sc.wait_ge(dv.h, dv_before)
        ac.inc(sc.activation(dst_stat[:, 1:2], dst_stat[:, 1:2], AF.Sqrt))

    for g in range(GT):
        @blk.sync
        def _(sy, g=g):
            if g >= 2:
                sy.wait_ge(dv.h, k._xn_done[g - 2])
            qs.inc(sy.dma_start(out=xf[g % 2][:],
                                in_=x_full[g * 128:(g + 1) * 128, :]), 16)

        if g == 0:
            k._xn_done = {}

        qs_after_load = qs.n

        @blk.vector
        def _(ve, g=g, qs_after_load=qs_after_load):
            ve.wait_ge(qs.h, qs_after_load)
            tile_stats(ve, xf[g % 2][:], stat)
            ve.wait_ge(dv.h, dv.n)
            if g >= 1:
                ve.wait_ge(ac.h, k._xnT_evac[g - 1])
            dv.inc(ve.tensor_scalar(xn_tm[:], xf[g % 2][:], stat[:, 0:1],
                                    stat[:, 1:2], ALU.subtract, ALU.mult))
            k._xn_done[g] = dv.n

        @blk.scalar
        def _(sc, g=g):
            tile_stats_sqrt(sc, stat, dv.n - 2)

        if g == 0:
            k._xnT_evac = {}
        dv_xn = dv.n
        pe_t0, ac_t0 = pe.n, ac.n

        @blk.tensor
        def _(t, g=g, dv_xn=dv_xn, pe_t0=pe_t0, ac_t0=ac_t0):
            t.wait_ge(dv.h, dv_xn)
            for grp in range(4):
                if grp >= 1:
                    t.wait_ge(ac.h, ac_t0 + grp)
                for s4 in range(4):
                    kk = grp * 4 + s4
                    i = t.transpose(k.pt_bf[:, s4 * 128:(s4 + 1) * 128],
                                    xn_tm[:, kk * 128:(kk + 1) * 128], k.ident[:])
                pe.inc(i)

        @blk.scalar
        def _(sc, g=g, pe_t0=pe_t0):
            for grp in range(4):
                sc.wait_ge(pe.h, pe_t0 + grp + 1)
                ac.inc(sc.copy(xnT[g % 2][:, grp * 512:(grp + 1) * 512],
                               k.pt_bf[:]))
            k._xnT_evac[g] = ac.n

        ac_xnT = ac.n

        @blk.tensor
        def _(t, g=g, ac_xnT=ac_xnT):
            t.wait_ge(ac.h, ac_xnT)
            if g >= 1:
                t.wait_ge(dv.h, k._ctm_read[g - 1])
            for kk in range(DC):
                i = t.matmul(k.pp[0][:], xnT[g % 2][:, kk * 128:(kk + 1) * 128],
                             wdkvT[kk][:], start=(kk == 0), stop=(kk == DC - 1))
            pe.inc(i)

        pe_c = pe.n
        if g == 0:
            k._ctm_read = {}
            k._ckv_done = {}
            k._v_done = {}
            k._v_evac = {}
            k._v_dma = {}
            k._ckv_ev = {}

        if g >= 1:
            k._emit_v(g - 1)

        @blk.vector
        def _(ve, g=g, pe_c=pe_c):
            ve.wait_ge(pe.h, pe_c)
            for gq in range(1):
                dv.inc(ve.bn_stats(mv6[:, 0:6], k.pp[0][:, 0:512]))
            ve.wait_ge(dv.h, dv.n)
            dv.inc(ve.bn_aggr(mv2[:], mv6[:, 0:6]))
            ve.wait_ge(dv.h, dv.n)
            dv.inc(ve.tensor_scalar_add(cstat[:, 1:2], mv2[:, 1:2], LN_EPS))
            dv.inc(ve.tensor_copy(cstat[:, 0:1], mv2[:, 0:1]))
            ve.wait_ge(dv.h, dv.n)
            ve.wait_ge(ac.h, ac.n + 1)
            dv.inc(ve.reciprocal(cstat[:, 1:2], cstat[:, 1:2]))
            ve.wait_ge(dv.h, dv.n)
            if g >= 1:
                ve.wait_ge(ac.h, k._ckv_done[g - 1])
            dv.inc(ve.tensor_scalar(ctm[:], k.pp[0][:, 0:512], cstat[:, 0:1],
                                    cstat[:, 1:2], ALU.subtract, ALU.mult))
            ve.wait_ge(dv.h, dv.n)
            dv.inc(ve.tensor_mul(ctm[:], ctm[:], gbc[:]))
            ve.wait_ge(dv.h, dv.n)
            dv.inc(ve.tensor_add(ctm[:], ctm[:], bbc[:]))
            k._ctm_read[g] = dv.n

        @blk.scalar
        def _(sc, g=g):
            sc.wait_ge(dv.h, dv.n - 4)
            ac.inc(sc.activation(cstat[:, 1:2], cstat[:, 1:2], AF.Sqrt))


        dv_ctm = dv.n
        pe_ct0, ac_ct0 = pe.n, ac.n

        @blk.tensor
        def _(t, g=g, dv_ctm=dv_ctm):
            t.wait_ge(dv.h, dv_ctm)
            for s4 in range(4):
                i = t.transpose(k.pt_bf[:, s4 * 128:(s4 + 1) * 128],
                                ctm[:, s4 * 128:(s4 + 1) * 128], k.ident[:])
            pe.inc(i)

        pe_ctr = pe.n

        @blk.scalar
        def _(sc, g=g, pe_ctr=pe_ctr):
            sc.wait_ge(pe.h, pe_ctr)
            for c in range(LC):
                i = sc.copy(k.ckv[c][:, g * 128:(g + 1) * 128],
                            k.pt_bf[:, c * 128:(c + 1) * 128])
            ac.inc(i)
            k._ckv_done[g] = ac.n

        ac_ckv = ac.n
        k._ckv_ev[g] = ac_ckv

        def emit_v(gv):
            @blk.tensor
            def _(t, gv=gv):
                t.wait_ge(ac.h, k._ckv_ev[gv])
                if gv >= 1:
                    t.wait_ge(ac.h, k._v_evac[gv - 1])
                for c in range(LC):
                    for dg in range(4):
                        i = t.matmul(k.pp[1 + dg][:],
                                     k.ckv[c][:, gv * 128:(gv + 1) * 128],
                                     wuvT[c][:, dg * 512:(dg + 1) * 512],
                                     start=(c == 0), stop=(c == LC - 1))
                pe.inc(i)
                k._v_done[gv] = pe.n

            pe_v = pe.n

            @blk.scalar
            def _(sc, gv=gv, pe_v=pe_v):
                sc.wait_ge(pe.h, pe_v)
                if gv >= 2:
                    sc.wait_ge(qs.h, k._v_dma[gv - 2])
                for dg in range(4):
                    i = sc.copy(v_sb[gv % 2][:, dg * 512:(dg + 1) * 512],
                                k.pp[1 + dg][:])
                ac.inc(i)
                k._v_evac[gv] = ac.n

            ac_v = ac.n

            @blk.sync
            def _(sy, gv=gv, ac_v=ac_v):
                sy.wait_ge(ac.h, ac_v)
                qs.inc(sy.dma_start(out=v_dram[gv], in_=v_sb[gv % 2][:]), 16)
                k._v_dma[gv] = qs.n

        k._emit_v = emit_v
        if g == GT - 1:
            emit_v(g)

    k.barrier()

    @blk.sync
    def _(sy):
        for c in range(LC):
            qs.inc(sy.dma_start(out=ckv_out[c * 128:(c + 1) * 128, :],
                                in_=k.ckv[c][:]), 16)

    if stop_after == "ckv":
        k.barrier()
        return None

    k.xn = [k.sbuf(f"xn{kk}", [128, TOK], BF16) for kk in range(DC)]
    for j in range(NT):
        @blk.sync
        def _(sy, j=j):
            if j >= 2:
                sy.wait_ge(dv.h, k._xn_done[GT + j - 2])
            qs.inc(sy.dma_start(out=xf[j % 2][:],
                                in_=x_own[j * 128:(j + 1) * 128, :]), 16)

        qs_l = qs.n

        @blk.vector
        def _(ve, j=j, qs_l=qs_l):
            ve.wait_ge(qs.h, qs_l)
            tile_stats(ve, xf[j % 2][:], stat)
            ve.wait_ge(dv.h, dv.n)
            if j >= 1:
                ve.wait_ge(ac.h, k._xnT_evac[GT + j - 1])
            dv.inc(ve.tensor_scalar(xn_tm[:], xf[j % 2][:], stat[:, 0:1],
                                    stat[:, 1:2], ALU.subtract, ALU.mult))
            k._xn_done[GT + j] = dv.n

        @blk.scalar
        def _(sc, j=j):
            tile_stats_sqrt(sc, stat, dv.n - 2)

        dv_xn = dv.n
        pe_t0, ac_t0 = pe.n, ac.n

        @blk.tensor
        def _(t, j=j, dv_xn=dv_xn, pe_t0=pe_t0, ac_t0=ac_t0):
            t.wait_ge(dv.h, dv_xn)
            for grp in range(4):
                if grp >= 1:
                    t.wait_ge(ac.h, ac_t0 + grp)
                for s4 in range(4):
                    kk = grp * 4 + s4
                    i = t.transpose(k.pt_bf[:, s4 * 128:(s4 + 1) * 128],
                                    xn_tm[:, kk * 128:(kk + 1) * 128], k.ident[:])
                pe.inc(i)

        @blk.scalar
        def _(sc, j=j, pe_t0=pe_t0):
            for grp in range(4):
                sc.wait_ge(pe.h, pe_t0 + grp + 1)
                for s4 in range(4):
                    kk = grp * 4 + s4
                    i = sc.copy(k.xn[kk][:, j * 128:(j + 1) * 128],
                                k.pt_bf[:, s4 * 128:(s4 + 1) * 128])
                ac.inc(i)
            k._xnT_evac[GT + j] = ac.n

    k.barrier()

    @blk.sync
    def _(sy):
        for kk in range(DC):
            qs.inc(sy.dma_start(out=xn_out[kk * 128:(kk + 1) * 128, :],
                                in_=k.xn[kk][:]), 16)

    k.barrier()
    return None


def emit_attn(k: K, standalone=True, stop_after=None):
    nc, blk = k.nc, k.block
    qs, qp, pe, dv, ac, po = k.qs, k.qp, k.pe, k.dv, k.ac, k.po
    s_sc = Sem(nc)
    s_pt = Sem(nc)
    s_vu = Sem(nc)
    s_cx = Sem(nc)
    s_ex = Sem(nc)

    wq = k.dram("wq", [D, D], F32, "ExternalInput")
    wuk = k.dram("wuk", [D, LAT], F32, "ExternalInput")
    wo = k.dram("wo", [D, D], F32, "ExternalInput")
    maskin = k.dram("maskin", [128, 1024], F32, "ExternalInput")
    satt_out = k.dram("satt_out", [D, TOK], BF16, "ExternalOutput")
    if standalone:
        v_dram = k.dram("v_dram", [GT, 128, D], BF16, "ExternalInput")
        xn_in = k.dram("xn_in", [D, TOK], BF16, "ExternalInput")
        ckv_in = k.dram("ckv_in", [LAT, S], BF16, "ExternalInput")
        k.emit_identity()
        k.ckv = [k.sbuf(f"ckv{c}", [128, S], BF16) for c in range(LC)]

        @blk.sync
        def _(sy):
            for c in range(LC):
                qs.inc(sy.dma_start(out=k.ckv[c][:],
                                    in_=ckv_in[c * 128:(c + 1) * 128, :]), 16)
        k.barrier()
    else:
        v_dram = k.v_dram

    ar = k.arena("attn_arena", 49152)
    mask = k.sbuf("attn_mask", [128, 1024], F32)
    wuk_sb = [k.sbuf_at(f"wuk{kk}", [128, LAT], BF16, ar + 1024 * kk)
              for kk in range(DC)]
    wstg = [k.sbuf_at(f"attn_wstg{i}", [128, D], BF16, ar + 16384 + 4096 * i)
            for i in range(2)]
    wqt = k.sbuf_at("attn_wqt", [128, D], BF16, ar + 24576)
    ak = [k.sbuf(f"ak{h}", [128, LAT], BF16) for h in range(NH)]

    @blk.sync
    def _(sy):
        qs.inc(sy.dma_start(out=mask[:], in_=maskin), 16)

    @blk.gpsimd
    def _(g):
        for kk in range(DC):
            qp.inc(g.dma_start(out=wuk_sb[kk][:],
                               in_=wuk[kk * 128:(kk + 1) * 128, :]), 16)
        for h in range(2):
            qp.inc(g.dma_start(out=wstg[h][:],
                               in_=wq[h * 128:(h + 1) * 128, :]), 16)

    k.barrier()

    ak_ev = {}
    for h in range(NH):
        pe_t0, ac_t0 = pe.n, ac.n

        @blk.tensor
        def _(t, h=h, pe_t0=pe_t0, ac_t0=ac_t0):
            if h >= 2:
                t.wait_ge(qp.h, k._wq_pf[h - 2])
            for grp in range(4):
                if grp >= 1:
                    t.wait_ge(ac.h, ac_t0 + grp)
                for s4 in range(4):
                    kk = grp * 4 + s4
                    i = t.transpose(k.pt_bf[:, s4 * 128:(s4 + 1) * 128],
                                    wstg[h % 2][:, kk * 128:(kk + 1) * 128],
                                    k.ident[:])
                pe.inc(i)

        @blk.scalar
        def _(sc, h=h, pe_t0=pe_t0):
            for grp in range(4):
                sc.wait_ge(pe.h, pe_t0 + grp + 1)
                ac.inc(sc.copy(wqt[:, grp * 512:(grp + 1) * 512], k.pt_bf[:]))

        ac_wqt = ac.n

        @blk.tensor
        def _(t, h=h, ac_wqt=ac_wqt):
            t.wait_ge(ac.h, ac_wqt)
            if h >= 1:
                t.wait_ge(ac.h, ak_ev[h - 1])
            for kk in range(DC):
                i = t.matmul(k.pp[5][:], wqt[:, kk * 128:(kk + 1) * 128],
                             wuk_sb[kk][:], start=(kk == 0), stop=(kk == DC - 1))
            pe.inc(i)

        pe_ak = pe.n

        @blk.scalar
        def _(sc, h=h, pe_ak=pe_ak):
            sc.wait_ge(pe.h, pe_ak)
            ac.inc(sc.copy(ak[h][:], k.pp[5][:]))
        ak_ev[h] = ac.n

        if h == 0:
            k._wq_pf = {}

        @blk.gpsimd
        def _(g, h=h, pe_t0=pe_t0):
            if h + 2 < NH:
                g.wait_ge(pe.h, pe_t0 + 4)
                qp.inc(g.dma_start(out=wstg[h % 2][:],
                                   in_=wq[(h + 2) * 128:(h + 3) * 128, :]), 16)
                k._wq_pf[h] = qp.n

    k.barrier()
    if stop_after == "ak":
        ak_dbg = k.dram("ak_dbg", [NH, 128, LAT], BF16, "ExternalOutput")

        @blk.sync
        def _(sy):
            for h in range(NH):
                qs.inc(sy.dma_start(out=ak_dbg[h], in_=ak[h][:]), 16)
        k.barrier()
        return None

    scores = k.sbuf_at("scores", [128, S], F32, ar)
    pstrip = k.sbuf_at("pstrip", [128, S], BF16, ar + 32768)
    tmpTh = [k.sbuf(f"tmpTh{h}", [128, LAT], BF16) for h in range(NH)]
    xnq = [k.sbuf(f"xnq{i}", [128, D], BF16) for i in range(2)]
    vring = [k.sbuf(f"vring{i}", [128, 1024], BF16) for i in range(3)]
    pTst = [k.sbuf(f"pTst{i}", [128, 512], BF16) for i in range(4)]
    ctx_fm = [k.sbuf(f"ctx{kk}", [128, TOK], BF16) for kk in range(DC)]
    mrow = k.sbuf("attn_mrow", [128, 1], F32)
    mrowA = k.sbuf("attn_mrowA", [128, 1], F32)
    negm = k.sbuf("attn_negm", [128, 1], F32)
    ellA = k.sbuf("attn_ellA", [128, 1], F32)
    ell = [k.sbuf(f"attn_ell{i}", [128, 1], F32) for i in range(2)]
    rinv = [k.sbuf(f"attn_rinv{i}", [128, 1], F32) for i in range(2)]
    ctx_tm = k.sbuf("ctx_tm", [128, 128], BF16)

    s_tq = Sem(nc)
    s_ptA = Sem(nc)
    s_ptD = Sem(nc)
    s_ptr = Sem(nc)
    s_cm = Sem(nc)
    s_xq = Sem(nc)

    sc_mm = {}
    exp_done = {}
    exp_half = {}
    cm_end = {}
    vdma = {}
    vfree = {}
    gvc_counter = [0]
    hidx = [0]

    heads = [(j, h) for j in range(NT) for h in range(NH)]
    only_batch = stop_after == "h0"
    only_qk = stop_after == "hq"
    if only_batch or only_qk:
        heads = heads[:1]
        stop_after = "ctx"
    elif stop_after and stop_after.startswith("h") and stop_after[1:].isdigit():
        heads = heads[:int(stop_after[1:])]
        stop_after = "ctx"
    cm_expect = {}
    acc = 0
    for (j, h) in heads:
        acc += 2 * j + 2
        cm_expect[(j, h)] = acc
    tq_expect = {}

    def emit_tmpT_batch(j):
        @blk.sync
        def _(sy, j=j):
            if j >= 2:
                sy.wait_ge(s_xq.h, j - 1)
            for kk in range(DC):
                qs.inc(sy.dma_start(
                    out=xnq[j % 2][:, kk * 128:(kk + 1) * 128],
                    in_=xn_in[kk * 128:(kk + 1) * 128,
                              j * 128:(j + 1) * 128]), 16)

        qs_xn = qs.n
        sc_prev = s_sc.n
        tq0 = s_tq.n

        @blk.tensor
        def _(t, j=j, qs_xn=qs_xn, sc_prev=sc_prev, tq0=tq0):
            t.wait_ge(qs.h, qs_xn)
            t.wait_ge(s_sc.h, sc_prev)
            i = None
            for r4 in range(4):
                for hi in range(4):
                    h = r4 * 4 + hi
                    if r4 >= 1:
                        t.wait_ge(s_tq.h, tq0 + 4 * (r4 - 1) + hi + 1)
                    for ls in range(LC):
                        i = t.matmul(k.pp[hi][:, ls * 128:(ls + 1) * 128],
                                     ak[h][:, ls * 128:(ls + 1) * 128],
                                     xnq[j % 2][:, h * 128:(h + 1) * 128],
                                     start=(ls == 0), stop=(ls == LC - 1))
                    s_ptr.inc(i)
            s_xq.inc(t.nop())

        ptr_b = s_ptr.n - 16

        @blk.vector
        def _(ve, j=j, ptr_b=ptr_b):
            for r4 in range(4):
                for hi in range(4):
                    h = r4 * 4 + hi
                    ve.wait_ge(s_ptr.h, ptr_b + r4 * 4 + hi + 1)
                    i = ve.tensor_copy(tmpTh[h][:], k.pp[hi][:])
                    s_tq.inc(i)
        tq_expect[j] = s_tq.n

    def emit_qk(j, h, prev):
        L = (8 * j + 8) * 128
        nkg = L // 512
        hj = (j, h)

        sc_base = s_sc.n

        @blk.tensor
        def _(t, j=j, h=h, nkg=nkg, sc_base=sc_base, prev=prev):
            t.wait_ge(s_tq.h, tq_expect[j])
            for kg in range(nkg):
                pb = k.pp[kg % 4]
                if kg >= 4:
                    t.wait_ge(s_sc.h, sc_base + kg - 3)
                elif prev is not None and kg == 0:
                    t.wait_ge(s_sc.h, sc_base)
                for ls in range(LC):
                    i = t.matmul(pb[:],
                                 tmpTh[h][:, ls * 128:(ls + 1) * 128],
                                 k.ckv[ls][:, kg * 512:(kg + 1) * 512],
                                 start=(ls == 0), stop=(ls == LC - 1))
                pe.inc(i)
                sc_mm[(j, h, kg)] = pe.n

        nh2 = nkg // 2

        @blk.vector
        def _(ve, j=j, h=h, nkg=nkg, L=L, prev=prev, nh2=nh2):
            if prev is not None:
                ve.wait_ge(s_ex.h, exp_done[prev])
            for kg in range(nkg):
                pb = k.pp[kg % 4]
                ve.wait_ge(pe.h, sc_mm[(j, h, kg)])
                if kg >= nkg - 2:
                    moff = (kg - (nkg - 2)) * 512
                    i = ve.scalar_tensor_tensor(
                        scores[:, kg * 512:(kg + 1) * 512], pb[:], 1.0,
                        mask[:, moff:moff + 512], ALU.mult, ALU.add)
                else:
                    i = ve.tensor_copy(scores[:, kg * 512:(kg + 1) * 512],
                                       pb[:])
                s_sc.inc(i)
                if kg == nh2 - 1 and nh2 >= 1:
                    ve.wait_ge(s_sc.h, s_sc.n)
                    dv.inc(ve.reduce_max(mrowA[:], scores[:, 0:nh2 * 512],
                                         axis=AX.X))

        def _max_part(j=j, h=h, L=L, ssc=s_sc.n, nh2=nh2):
            @blk.vector
            def _(ve):
                ve.wait_ge(s_sc.h, ssc)
                dv.inc(ve.reduce_max(mrow[:], scores[:, nh2 * 512:L],
                                     axis=AX.X))
                ve.wait_ge(dv.h, dv.n)
                if nh2 >= 1:
                    dv.inc(ve.tensor_max(mrow[:], mrow[:], mrowA[:]))
                    ve.wait_ge(dv.h, dv.n)
                dv.inc(ve.tensor_scalar_mul(negm[:], mrow[:], -QSCALE))
            return dv.n

        def _exp_part(dvn, j=j, h=h, L=L, prev=prev, hj=hj):
            @blk.scalar
            def _(sc):
                sc.wait_ge(dv.h, dvn)
                if prev is not None:
                    sc.wait_ge(s_cm.h, cm_expect[prev])
                half = (L // 1024) * 512
                i = sc.activation(pstrip[:, 0:half], scores[:, 0:half],
                                  AF.Exp, bias=negm[:], scale=QSCALE,
                                  accum_out=ellA[:])
                s_ex.inc(i)
                exp_half[hj] = s_ex.n
                i = sc.activation(pstrip[:, half:L], scores[:, half:L],
                                  AF.Exp, bias=negm[:], scale=QSCALE,
                                  accum_out=ell[h % 2][:])
                s_ex.inc(i)
            exp_done[hj] = s_ex.n
        return _max_part, _exp_part

    def emit_ctx(j, h):
        L = (8 * j + 8) * 128
        nkt = L // 128
        nvc = L // 1024
        nkq = nkt // 4
        hj = (j, h)

        gvc0 = gvc_counter[0]
        cm_base = s_cm.n
        for vc in range(nvc):
            vfree[gvc0 + vc] = cm_base + 2 * vc + 2

        @blk.sync
        def _(sy, j=j, h=h, nvc=nvc, gvc0=gvc0):
            for vc in range(nvc):
                gvc = gvc0 + vc
                if gvc >= 3:
                    sy.wait_ge(s_cm.h, vfree[gvc - 3])
                qs.inc(sy.dma_start(
                    out=vring[gvc % 3][:].rearrange("p (kt d) -> p kt d",
                                                    d=128),
                    in_=v_dram[8 * vc:8 * vc + 8, :, h * 128:(h + 1) * 128]
                    .rearrange("kt p d -> p kt d")), 16)
                vdma[gvc] = qs.n

        ptA_base = s_ptA.n
        ptD_base = s_ptD.n
        ptr_base = s_ptr.n

        @blk.tensor
        def _(t, j=j, h=h, nkq=nkq, nkt=nkt, gvc0=gvc0,
              ptA=ptA_base, ptD=ptD_base, sxh=exp_half[hj],
              sxn=exp_done[hj], hx=hidx[0]):
            t.wait_ge(s_ex.h, sxh)
            for kq in range(nkq):
                if kq == nkq // 2:
                    t.wait_ge(s_ex.h, sxn)
                even = (kq % 2 == 0)
                bank = k.pt_bf if even else k.pt_bf2
                sem, base = (s_ptA, ptA) if even else (s_ptD, ptD)
                if kq <= 1:
                    t.wait_ge(sem.h, base)
                else:
                    t.wait_ge(sem.h, base + kq // 2)
                for s4 in range(4):
                    kt = kq * 4 + s4
                    i = t.transpose(bank[:, s4 * 128:(s4 + 1) * 128],
                                    pstrip[:, kt * 128:(kt + 1) * 128],
                                    k.ident[:])
                s_ptr.inc(i)
                t.wait_ge(sem.h, base + kq // 2 + 1)
                t.wait_ge(qs.h, vdma[gvc0 + (kq * 4) // 8])
                if kq == 0:
                    t.wait_ge(s_cx.h, max(hx - 1, 0))
                cb = k.pp[4 + hx % 2]
                for s4 in range(4):
                    kt = kq * 4 + s4
                    vseg = vring[(gvc0 + kt // 8) % 3][:].rearrange(
                        "p (kt d) -> p kt d", d=128)[:, kt % 8, :]
                    i = t.matmul(cb[:, 0:128],
                                 pTst[kq % 4][:, s4 * 128:(s4 + 1) * 128],
                                 vseg, start=(kt == 0), stop=(kt == nkt - 1))
                s_cm.inc(i)
        assert s_cm.n == cm_expect[hj], (hj, s_cm.n, cm_expect[hj])
        gvc_counter[0] += nvc

        @blk.scalar
        def _(sc, nkq=nkq, ptr_base=ptr_base, cm_base=cm_base):
            for kq in range(0, nkq, 2):
                sc.wait_ge(s_ptr.h, ptr_base + kq + 1)
                if kq >= 4:
                    sc.wait_ge(s_cm.h, cm_base + kq - 3)
                i = sc.copy(pTst[kq % 4][:], k.pt_bf[:])
                s_ptA.inc(i)

        @blk.vector
        def _(ve, h=h, nkq=nkq, ptr_base=ptr_base, cm_base=cm_base,
              sxn=exp_done[hj]):
            ve.wait_ge(s_ex.h, sxn)
            dv.inc(ve.tensor_add(ell[h % 2][:], ell[h % 2][:], ellA[:]))
            ve.wait_ge(dv.h, dv.n)
            dv.inc(ve.reciprocal(rinv[h % 2][:], ell[h % 2][:]))
            for kq in range(1, nkq, 2):
                ve.wait_ge(s_ptr.h, ptr_base + kq + 1)
                if kq >= 4:
                    ve.wait_ge(s_cm.h, cm_base + kq - 3)
                i = ve.tensor_copy(pTst[kq % 4][:], k.pt_bf2[:])
                s_ptD.inc(i)
        dv_rinv = dv.n

        @blk.scalar
        def _(sc, h=h, cme=cm_expect[hj], dvr=dv_rinv):
            sc.wait_ge(s_cm.h, cme)
            sc.wait_ge(dv.h, dvr)
            i = sc.activation(ctx_tm[:], k.pp[4 + (hidx[0]) % 2][:, 0:128], AF.Copy,
                              scale=rinv[h % 2][:])
            s_cx.inc(i)
        hidx[0] += 1

        @blk.tensor
        def _(t, hx=hidx[0], spn=s_ptA.n):
            t.wait_ge(s_cx.h, hx)
            t.wait_ge(s_ptA.h, spn)
            i = t.transpose(k.pt_bf[:, 0:128], ctx_tm[:], k.ident[:])
            s_ptr.inc(i)

        @blk.scalar
        def _(sc, j=j, h=h, sptr=s_ptr.n):
            sc.wait_ge(s_ptr.h, sptr)
            i = sc.copy(ctx_fm[h][:, j * 128:(j + 1) * 128],
                        k.pt_bf[:, 0:128])
            s_ptA.inc(i)

    prev = None
    pending = None
    for hj in heads:
        if hj[1] == 0:
            emit_tmpT_batch(hj[0])
        if only_batch:
            break
        max_part, exp_part = emit_qk(hj[0], hj[1], prev)
        if pending is not None:
            emit_ctx(*pending)
        dvn = max_part()
        exp_part(dvn)
        prev = hj
        pending = hj
    if pending is not None and not only_qk:
        emit_ctx(*pending)

    k.barrier()
    if stop_after == "ctx":
        ctx_dbg = k.dram("ctx_dbg", [D, TOK], BF16, "ExternalOutput")

        @blk.sync
        def _(sy):
            for kk in range(DC):
                qs.inc(sy.dma_start(out=ctx_dbg[kk * 128:(kk + 1) * 128, :],
                                    in_=ctx_fm[kk][:]), 16)
        k.barrier()
        return None

    wot = wqt
    satt_sb = [k.sbuf_at(f"satt{i}", [128, TOK], BF16, ar + 28672 + 2048 * i)
               for i in range(2)]
    wo_pf = {}

    @blk.gpsimd
    def _(g):
        for oc in range(2):
            qp.inc(g.dma_start(out=wstg[oc][:],
                               in_=wo[oc * 128:(oc + 1) * 128, :]), 16)
            wo_pf[oc] = qp.n

    for oc in range(DC):
        pe_t0, ac_t0 = pe.n, ac.n

        @blk.tensor
        def _(t, oc=oc, pe_t0=pe_t0, ac_t0=ac_t0):
            t.wait_ge(qp.h, wo_pf[oc])
            for grp in range(4):
                if grp >= 1:
                    t.wait_ge(ac.h, ac_t0 + grp)
                for s4 in range(4):
                    kk = grp * 4 + s4
                    i = t.transpose(k.pt_bf[:, s4 * 128:(s4 + 1) * 128],
                                    wstg[oc % 2][:, kk * 128:(kk + 1) * 128],
                                    k.ident[:])
                pe.inc(i)

        @blk.scalar
        def _(sc, oc=oc, pe_t0=pe_t0):
            for grp in range(4):
                sc.wait_ge(pe.h, pe_t0 + grp + 1)
                ac.inc(sc.copy(wot[:, grp * 512:(grp + 1) * 512], k.pt_bf[:]))

        ac_wot = ac.n

        @blk.gpsimd
        def _(g, oc=oc, pe_t0=pe_t0):
            if oc + 2 < DC:
                g.wait_ge(pe.h, pe_t0 + 4)
                qp.inc(g.dma_start(out=wstg[oc % 2][:],
                                   in_=wo[(oc + 2) * 128:(oc + 3) * 128, :]), 16)
                wo_pf[oc + 2] = qp.n

        @blk.tensor
        def _(t, oc=oc, ac_wot=ac_wot):
            t.wait_ge(ac.h, ac_wot)
            if oc >= 1:
                t.wait_ge(ac.h, satt_ev[oc - 1])
            for qg in range(2):
                for kk in range(DC):
                    i = t.matmul(k.pp[qg][:],
                                 wot[:, kk * 128:(kk + 1) * 128],
                                 ctx_fm[kk][:, qg * 512:(qg + 1) * 512],
                                 start=(kk == 0), stop=(kk == DC - 1))
                pe.inc(i)

        pe_so = pe.n
        if oc == 0:
            satt_ev = {}

        @blk.scalar
        def _(sc, oc=oc, pe_so=pe_so):
            for qg in range(2):
                sc.wait_ge(pe.h, pe_so - 2 + qg + 1)
                if oc >= 2:
                    sc.wait_ge(qs.h, satt_dma[oc - 2])
                ac.inc(sc.copy(satt_sb[oc % 2][:, qg * 512:(qg + 1) * 512],
                               k.pp[qg][:]))
        satt_ev[oc] = ac.n
        ac_so = ac.n
        if oc == 0:
            satt_dma = {}

        @blk.sync
        def _(sy, oc=oc, ac_so=ac_so):
            sy.wait_ge(ac.h, ac_so)
            qs.inc(sy.dma_start(out=satt_out[oc * 128:(oc + 1) * 128, :],
                                in_=satt_sb[oc % 2][:]), 16)
        satt_dma[oc] = qs.n

    k.barrier()
    return None


def emit_ffn(k: K, standalone=True, stop_after=None):
    nc, blk = k.nc, k.block
    qs, qp, pe, dv, ac, po = k.qs, k.qp, k.pe, k.dv, k.ac, k.po
    s_q = Sem(nc)
    s_tr = Sem(nc)
    s_ev = Sem(nc)
    s_evD = Sem(nc)
    s_hd = Sem(nc)
    s_gf = Sem(nc)

    w1 = k.dram("w1", [FFN_H, D], F32, "ExternalInput")
    w2 = k.dram("w2", [FFN_H, D], F32, "ExternalInput")
    w3 = k.dram("w3", [D, FFN_H], F32, "ExternalInput")
    x_ownT = k.dram("x_ownT", [D, TOK], F32, "ExternalInput")
    x1_out = k.dram("x1_out", [D, TOK], F32, "ExternalOutput")
    if standalone:
        xn_in = k.dram("xn_in", [D, TOK], BF16, "ExternalInput")
        satt_in = k.dram("satt_in", [D, TOK], BF16, "ExternalInput")
        k.emit_identity()
        k.xn = [k.sbuf(f"xn{kk}", [128, TOK], BF16) for kk in range(DC)]

        @blk.sync
        def _(sy):
            for kk in range(DC):
                qs.inc(sy.dma_start(out=k.xn[kk][:],
                                    in_=xn_in[kk * 128:(kk + 1) * 128, :]), 16)
        k.barrier()
    else:
        satt_in = k.satt_dram

    wstg = [k.sbuf(f"ffn_wstg{i}", [128, D], F32) for i in range(4)]
    gcols = k.sbuf("ffn_gcols", [128, 136], F32)
    qa = k.sbuf("ffn_qa", [128, D], F32)
    ar2 = k.arena("ffn_qarena", 24576)
    qbf = [k.sbuf_at(f"ffn_qbf{i}", [128, D], BF16, ar2 + 4096 * i)
           for i in range(2)]
    qT = [k.sbuf_at(f"ffn_qT{i}", [128, D], BF16, ar2 + 8192 + 4096 * i)
          for i in range(2)]
    hidden = [k.sbuf(f"hid{hc}", [128, TOK], BF16) for hc in range(HC)]
    gfm = [k.sbuf_at(f"gfm{kk}", [128, TOK], BF16,
                     k.nc.lookup_mloc(k.xn[kk]).addr) for kk in range(DC)]
    tau = [k.sbuf(f"tau{i}", [128, 1], F32) for i in range(3)]
    ntau = [k.sbuf(f"ntau{i}", [128, 1], F32) for i in range(3)]
    grow = k.sbuf("ffn_grow", [1, 128], F32)
    g11 = k.sbuf("ffn_g11", [1, 1], F32)
    sgt = k.sbuf("ffn_sgt", [128, TOK], F32)
    qbf3 = k.sbuf_at("ffn_qbf3", [128, FFN_H], BF16, ar2)
    q3T = k.sbuf_at("ffn_q3T", [128, FFN_H], BF16, ar2 + 11264)
    xstg = [k.sbuf(f"ffn_xstg{i}", [128, TOK], F32) for i in range(2)]
    sstg = [k.sbuf(f"ffn_sstg{i}", [128, TOK], BF16) for i in range(2)]

    jobs = []
    for hc in range(HC):
        jobs.append((w1[hc * 128:(hc + 1) * 128, :], len(jobs)))
    for hc in range(HC):
        jobs.append((w2[hc * 128:(hc + 1) * 128, :], len(jobs)))
    w3cols = [(0, 2048), (2048, 4096), (4096, 5632)]
    for dc in range(DC):
        for (c0, c1) in w3cols:
            jobs.append((w3[dc * 128:(dc + 1) * 128, c0:c1], len(jobs)))

    qs_g0, qp_g0, ac_g0 = qs.n, qp.n, ac.n

    @blk.sync
    def _(sy):
        for ji, (src, ci) in enumerate(jobs):
            if ji % 2 == 0:
                if ji >= 4:
                    sy.wait_ge(ac.h, ac_g0 + ji - 3)
                w = src.shape[-1]
                qs.inc(sy.dma_start(out=wstg[ji % 4][:, 0:w], in_=src), 16)

    @blk.gpsimd
    def _(g):
        for ji, (src, ci) in enumerate(jobs):
            if ji % 2 == 1:
                if ji >= 4:
                    g.wait_ge(ac.h, ac_g0 + ji - 3)
                w = src.shape[-1]
                qp.inc(g.dma_start(out=wstg[ji % 4][:, 0:w], in_=src), 16)

    @blk.scalar
    def _(sc):
        nq = {0: qs_g0, 1: qp_g0}
        seen = {0: 0, 1: 0}
        for ji, (src, ci) in enumerate(jobs):
            par = ji % 2
            seen[par] += 1
            sc.wait_ge([qs.h, qp.h][par], [qs_g0, qp_g0][par] + 16 * seen[par])
            w = src.shape[-1]
            ac.inc(sc.activation(qa[:, 0:w], wstg[ji % 4][:, 0:w], AF.Abs,
                                 accum_out=gcols[:, ci:ci + 1]))

    k.barrier()
    if stop_after == "gstream":
        gdbg2 = k.dram("gcols_dbg", [128, 136], F32, "ExternalOutput")

        @blk.sync
        def _(sy):
            qs.inc(sy.dma_start(out=gdbg2, in_=gcols[:]), 16)
        k.barrier()
        return None

    NW = float(FFN_H * D)
    ranges = [(0, HC), (HC, 2 * HC), (2 * HC, 2 * HC + 3 * DC)]
    for mi, (c0, c1) in enumerate(ranges):
        @blk.vector
        def _(ve, mi=mi, c0=c0, c1=c1):
            dv.inc(ve.reduce_sum(tau[mi][:], gcols[:, c0:c1], axis=AX.X))

        k.barrier()

        @blk.tensor
        def _(t, mi=mi):
            pe.inc(t.transpose(k.pt32[0:1, 0:128], tau[mi][:],
                               k.ident32[:]))

        @blk.scalar
        def _(sc, mi=mi):
            pe.wait(sc)
            ac.inc(sc.copy(grow[:], k.pt32[0:1, 0:128]))

        k.barrier()

        @blk.vector
        def _(ve, mi=mi):
            dv.inc(ve.reduce_sum(g11[:], grow[:], axis=AX.X))
            ve.wait_ge(dv.h, dv.n)
            s0 = bass.AP(tensor=g11.ap().tensor, offset=0, ap=[[1, 1], [0, 128]])
            dv.inc(ve.tensor_copy(qa[0:1, 0:128], s0))

        k.barrier()

        @blk.tensor
        def _(t, mi=mi):
            pe.inc(t.transpose(k.pt32[:], qa[:, 0:128], k.ident32[:]))

        @blk.scalar
        def _(sc, mi=mi):
            pe.wait(sc)
            ac.inc(sc.copy(tau[mi][:], k.pt32[:, 0:1]))

        k.barrier()

        @blk.vector
        def _(ve, mi=mi):
            dv.inc(ve.tensor_scalar(tau[mi][:], tau[mi][:], 0.5 / NW, 0.5e-8,
                                    ALU.mult, ALU.add))
            ve.wait_ge(dv.h, dv.n)
            dv.inc(ve.tensor_scalar_mul(ntau[mi][:], tau[mi][:], -1.0))

    k.barrier()
    if stop_after == "gamma":
        gdbg = k.dram("gamma_dbg", [3, 128, 1], F32, "ExternalOutput")

        @blk.sync
        def _(sy):
            for mi in range(3):
                qs.inc(sy.dma_start(out=gdbg[mi], in_=tau[mi][:]), 16)
        k.barrier()
        return None

    ld_w = {}
    qtr_done = {}
    for hc in range(HC):
        @blk.sync
        def _(sy, hc=hc):
            if hc >= 2:
                sy.wait_ge(s_q.h, 2 * (hc - 1))
            qs.inc(sy.dma_start(out=wstg[(2 * hc) % 4][:],
                                in_=w1[hc * 128:(hc + 1) * 128, :]), 16)
            ld_w[(hc, 0)] = qs.n

        @blk.gpsimd
        def _(g, hc=hc):
            if hc >= 2:
                g.wait_ge(s_q.h, 2 * (hc - 1))
            qp.inc(g.dma_start(out=wstg[(2 * hc + 1) % 4][:],
                               in_=w2[hc * 128:(hc + 1) * 128, :]), 16)
            ld_w[(hc, 1)] = qp.n

        for wi in range(2):
            @blk.vector
            def _(ve, hc=hc, wi=wi):
                ve.wait_ge([qs.h, qp.h][wi], ld_w[(hc, wi)])
                if hc >= 1:
                    ve.wait_ge(s_tr.h, qtr_done[(hc - 1, wi)])
                stg = wstg[(2 * hc + wi) % 4]
                i = ve.tensor_scalar(qa[:], stg[:], ntau[wi][:], None, ALU.is_lt)
                dv.inc(i)
                ve.wait_ge(dv.h, dv.n)
                i = ve.scalar_tensor_tensor(qbf[wi][:], stg[:], tau[wi][:],
                                            qa[:], ALU.is_gt, ALU.subtract)
                s_q.inc(i)

            sq_now = s_q.n
            tr0 = s_tr.n
            qtr_done[(hc, wi)] = tr0 + 4
            evA0 = s_ev.n
            evD0 = s_evD.n

            @blk.tensor
            def _(t, hc=hc, wi=wi, sq_now=sq_now, tr0=tr0, evA0=evA0,
                  evD0=evD0):
                t.wait_ge(s_q.h, sq_now)
                for grp in range(4):
                    even = grp % 2 == 0
                    bank = k.pt_bf if even else k.pt_bf2
                    sem, b0 = (s_ev, evA0) if even else (s_evD, evD0)
                    t.wait_ge(sem.h, b0 + grp // 2)
                    for s4 in range(4):
                        kk = grp * 4 + s4
                        i = t.transpose(bank[:, s4 * 128:(s4 + 1) * 128],
                                        qbf[wi][:, kk * 128:(kk + 1) * 128],
                                        k.ident[:])
                    s_tr.inc(i)

            @blk.scalar
            def _(sc, hc=hc, wi=wi, tr0=tr0):
                for grp in (0, 2):
                    sc.wait_ge(s_tr.h, tr0 + grp + 1)
                    i = sc.copy(qT[wi][:, grp * 512:(grp + 1) * 512], k.pt_bf[:])
                    s_ev.inc(i)

            @blk.vector
            def _(ve, hc=hc, wi=wi, tr0=tr0):
                for grp in (1, 3):
                    ve.wait_ge(s_tr.h, tr0 + grp + 1)
                    i = ve.tensor_copy(qT[wi][:, grp * 512:(grp + 1) * 512],
                                       k.pt_bf2[:])
                    s_evD.inc(i)

            evA_end = s_ev.n
            evD_end = s_evD.n

            @blk.tensor
            def _(t, hc=hc, wi=wi, evA_end=evA_end, evD_end=evD_end):
                t.wait_ge(s_ev.h, evA_end)
                t.wait_ge(s_evD.h, evD_end)
                if hc >= 1 or wi == 1:
                    t.wait_ge(s_hd.h, s_hd.n)
                for qg in range(2):
                    for kk in range(DC):
                        i = t.matmul(k.pp[2 * wi + qg][:],
                                     qT[wi][:, kk * 128:(kk + 1) * 128],
                                     k.xn[kk][:, qg * 512:(qg + 1) * 512],
                                     start=(kk == 0), stop=(kk == DC - 1))
                    pe.inc(i)

        pe_x12 = pe.n

        @blk.scalar
        def _(sc, hc=hc, pe_x12=pe_x12):
            sc.wait_ge(pe.h, pe_x12)
            for qg in range(2):
                ac.inc(sc.activation(sgt[:, qg * 512:(qg + 1) * 512],
                                     k.pp[qg][:], AF.Sigmoid))

        ac_sig = ac.n

        @blk.vector
        def _(ve, hc=hc, ac_sig=ac_sig, pe_x12=pe_x12):
            ve.wait_ge(ac.h, ac_sig)
            ve.wait_ge(pe.h, pe_x12)
            for qg in range(2):
                i = ve.tensor_mul(hidden[hc][:, qg * 512:(qg + 1) * 512],
                                  sgt[:, qg * 512:(qg + 1) * 512],
                                  k.pp[2 + qg][:])
                s_hd.inc(i)

    k.barrier()
    if stop_after == "hidden":
        hdbg = k.dram("hidden_dbg", [FFN_H, TOK], BF16, "ExternalOutput")

        @blk.sync
        def _(sy):
            for hc in range(HC):
                qs.inc(sy.dma_start(out=hdbg[hc * 128:(hc + 1) * 128, :],
                                    in_=hidden[hc][:]), 16)
        k.barrier()
        return None

    for dc in range(DC):
        @blk.sync
        def _(sy, dc=dc):
            if dc >= 1:
                sy.wait_ge(s_q.h, s_q.n)
            for wci, (c0, c1) in enumerate(w3cols):
                qs.inc(sy.dma_start(out=wstg[wci][:, 0:c1 - c0],
                                    in_=w3[dc * 128:(dc + 1) * 128, c0:c1]), 16)
            ld_w[(dc, 3)] = qs.n

        @blk.vector
        def _(ve, dc=dc):
            ve.wait_ge(qs.h, ld_w[(dc, 3)])
            if dc >= 1:
                ve.wait_ge(s_tr.h, s_tr.n)
            for wci, (c0, c1) in enumerate(w3cols):
                w = c1 - c0
                i = ve.tensor_scalar(qa[:, 0:w], wstg[wci][:, 0:w],
                                     ntau[2][:], None, ALU.is_lt)
                dv.inc(i)
                ve.wait_ge(dv.h, dv.n)
                i = ve.scalar_tensor_tensor(
                    qbf3[:, c0:c1], wstg[wci][:, 0:w], tau[2][:],
                    qa[:, 0:w], ALU.is_gt, ALU.subtract)
                s_q.inc(i)

        sq_now = s_q.n
        tr0 = s_tr.n
        evA0 = s_ev.n
        evD0 = s_evD.n

        @blk.tensor
        def _(t, dc=dc, sq_now=sq_now, tr0=tr0, evA0=evA0, evD0=evD0):
            t.wait_ge(s_q.h, sq_now)
            for grp in range(11):
                even = grp % 2 == 0
                bank = k.pt_bf if even else k.pt_bf2
                sem, b0 = (s_ev, evA0) if even else (s_evD, evD0)
                t.wait_ge(sem.h, b0 + grp // 2)
                for s4 in range(4):
                    hc = grp * 4 + s4
                    i = t.transpose(bank[:, s4 * 128:(s4 + 1) * 128],
                                    qbf3[:, hc * 128:(hc + 1) * 128],
                                    k.ident[:])
                s_tr.inc(i)

        @blk.scalar
        def _(sc, dc=dc, tr0=tr0):
            for grp in range(0, 11, 2):
                sc.wait_ge(s_tr.h, tr0 + grp + 1)
                i = sc.copy(q3T[:, grp * 512:(grp + 1) * 512], k.pt_bf[:])
                s_ev.inc(i)

        @blk.vector
        def _(ve, dc=dc, tr0=tr0):
            for grp in range(1, 11, 2):
                ve.wait_ge(s_tr.h, tr0 + grp + 1)
                i = ve.tensor_copy(q3T[:, grp * 512:(grp + 1) * 512],
                                   k.pt_bf2[:])
                s_evD.inc(i)

        evA_end = s_ev.n
        evD_end = s_evD.n

        @blk.tensor
        def _(t, dc=dc, evA_end=evA_end, evD_end=evD_end):
            t.wait_ge(s_ev.h, evA_end)
            t.wait_ge(s_evD.h, evD_end)
            if dc >= 1:
                t.wait_ge(s_gf.h, s_gf.n)
            for qg in range(2):
                for hc in range(HC):
                    i = t.matmul(k.pp[qg][:],
                                 q3T[:, hc * 128:(hc + 1) * 128],
                                 hidden[hc][:, qg * 512:(qg + 1) * 512],
                                 start=(hc == 0), stop=(hc == HC - 1))
                pe.inc(i)

        pe_f = pe.n

        @blk.scalar
        def _(sc, dc=dc, pe_f=pe_f):
            sc.wait_ge(pe.h, pe_f)
            for qg in range(2):
                i = sc.activation(gfm[dc][:, qg * 512:(qg + 1) * 512],
                                  k.pp[qg][:], AF.Silu)
                s_gf.inc(i)

    k.barrier()

    cmb = {}
    for dc in range(DC):
        @blk.sync
        def _(sy, dc=dc):
            if dc >= 2:
                sy.wait_ge(dv.h, cmb[dc - 2])
            qs.inc(sy.dma_start(out=xstg[dc % 2][:],
                                in_=x_ownT[dc * 128:(dc + 1) * 128, :]), 16)
            if standalone:
                qs.inc(sy.dma_start(out=sstg[dc % 2][:],
                                    in_=satt_in[dc * 128:(dc + 1) * 128, :]), 16)
            else:
                qs.inc(sy.dma_start(out=sstg[dc % 2][:],
                                    in_=satt_in[dc * 128:(dc + 1) * 128, :]), 16)
            ld_w[(dc, 9)] = qs.n

        qs_c = qs.n
        dma_c = {}

        @blk.vector
        def _(ve, dc=dc, qs_c=qs_c):
            ve.wait_ge(qs.h, qs_c)
            i = ve.tensor_mul(sgt[:], sstg[dc % 2][:], gfm[dc][:])
            dv.inc(i)
            ve.wait_ge(dv.h, dv.n)
            i = ve.tensor_add(xstg[dc % 2][:], xstg[dc % 2][:], sgt[:])
            dv.inc(i)
            cmb[dc] = dv.n

        @blk.sync
        def _(sy, dc=dc):
            sy.wait_ge(dv.h, cmb[dc])
            qs.inc(sy.dma_start(out=x1_out[dc * 128:(dc + 1) * 128, :],
                                in_=xstg[dc % 2][:]), 16)

    k.barrier()
    return None


def emit_l1_scan_chunks(k: K, whg, summ_pass, summ_out=None, hin=None,
                        hbf=None, sfx="", at=None):
    nc, blk = k.nc, k.block
    qs, qp, pe, dv, ac, po = k.qs, k.qp, k.pe, k.dv, k.ac, k.po
    s_wl = Sem(nc)

    if at is None:
        alloc = lambda n, s, d: k.sbuf(n, s, d)
    else:
        off = [at]

        def alloc(n, s, d, off=off):
            nbytes = (2 if d == BF16 else 4) * s[1]
            t = k.sbuf_at(n, s, d, off[0])
            off[0] += nbytes
            return t
    wstg = [alloc(f"l1w{sfx}{i}", [128, 4096], BF16) for i in range(2)]
    Abuf = alloc(f"l1A{sfx}", [128, TOK], F32)
    Vbuf = alloc(f"l1V{sfx}", [128, TOK], F32)
    sgA = alloc(f"l1s{sfx}", [128, TOK], F32)
    gtb = alloc(f"l1g{sfx}", [128, TOK], F32)
    mby = alloc(f"l1m{sfx}", [128, TOK], F32)
    zeros = alloc(f"l1z{sfx}", [128, 128], F32)
    Hloc = alloc(f"l1H{sfx}", [128, TOK], F32)
    Ploc = k.sbuf(f"l1P{sfx}", [128, TOK], F32) if summ_pass else None
    summ = k.sbuf(f"l1sm{sfx}", [128, 16], F32) if summ_pass else None

    @blk.gpsimd
    def _(g):
        g.memset(zeros[:], 0.0)
        po.inc(g.nop())

    k.barrier()

    ldw = {}
    hg_mm = {}
    for c in range(EC):
        @blk.gpsimd
        def _(g, c=c):
            if c >= 2:
                g.wait_ge(pe.h, hg_mm[c - 2])
            for dc in range(DC):
                src = bass.AP(tensor=whg.tensor,
                              offset=dc * 128 * 2 * DI + c * 128,
                              ap=[[2 * DI, 128], [EC * 128, 2], [1, 128]])
                qp.inc(g.dma_start(
                    out=wstg[c % 2][:, dc * 256:(dc + 1) * 256]
                    .rearrange("p (s w) -> p s w", s=2),
                    in_=src), 16)
            ldw[c] = qp.n

        @blk.tensor
        def _(t, c=c):
            t.wait_ge(qp.h, ldw[c])
            if c >= 1:
                t.wait_ge(dv.h, av_done[c - 1])
            for qg in range(2):
                for dc in range(DC):
                    i = t.matmul(k.pp[qg][:],
                                 wstg[c % 2][:, dc * 256:dc * 256 + 128],
                                 k.xn2[dc][:, qg * 512:(qg + 1) * 512],
                                 start=(dc == 0), stop=(dc == DC - 1))
                pe.inc(i)
                for dc in range(DC):
                    i = t.matmul(k.pp[2 + qg][:],
                                 wstg[c % 2][:, dc * 256 + 128:dc * 256 + 256],
                                 k.xn2[dc][:, qg * 512:(qg + 1) * 512],
                                 start=(dc == 0), stop=(dc == DC - 1))
                pe.inc(i)

        pe_hg = pe.n
        hg_mm[c] = pe.n
        if c == 0:
            av_done = {}
            ac_av = {}

        @blk.scalar
        def _(sc, c=c, pe_hg=pe_hg):
            for qg in range(2):
                sc.wait_ge(pe.h, pe_hg - 4 + 2 * qg + 2)
                if c >= 1:
                    sc.wait_ge(dv.h, av_done[c - 1])
                sl = slice(qg * 512, (qg + 1) * 512)
                ac.inc(sc.activation(Abuf[:, sl], k.pp[2 + qg][:], AF.Sigmoid,
                                     scale=-1.0))
                ac.inc(sc.activation(sgA[:, sl], k.pp[qg][:], AF.Sigmoid))
            ac_av[c] = ac.n

        @blk.vector
        def _(ve, c=c, pe_hg=pe_hg):
            for qg in range(2):
                sl = slice(qg * 512, (qg + 1) * 512)
                ve.wait_ge(ac.h, ac_av[c] - 2 * (1 - qg))
                ve.wait_ge(pe.h, pe_hg)
                ph = k.pp[qg]
                dv.inc(ve.tensor_single_scalar(mby[:, sl], ph[:], 0.0, ALU.is_ge))
                dv.inc(ve.tensor_single_scalar(gtb[:, sl], ph[:], 0.5, ALU.add))
                ve.wait_ge(dv.h, dv.n)
                dv.inc(ve.tensor_sub(gtb[:, sl], gtb[:, sl], sgA[:, sl]))
                ve.wait_ge(dv.h, dv.n)
                dv.inc(ve.tensor_mul(mby[:, sl], mby[:, sl], gtb[:, sl]))
                ve.wait_ge(dv.h, dv.n)
                dv.inc(ve.tensor_add(gtb[:, sl], sgA[:, sl], mby[:, sl]))
                ve.wait_ge(dv.h, dv.n)
                dv.inc(ve.tensor_scalar(mby[:, sl], Abuf[:, sl], -1.0, 1.0,
                                        ALU.mult, ALU.add))
                ve.wait_ge(dv.h, dv.n)
                dv.inc(ve.tensor_mul(Vbuf[:, sl], mby[:, sl], gtb[:, sl]))
                ve.wait_ge(dv.h, dv.n)
            av_done[c] = dv.n

        @blk.vector
        def _(ve, c=c):
            ve.wait_ge(dv.h, av_done[c])
            for j in range(NT):
                sl = slice(j * 128, (j + 1) * 128)
                init = 0.0 if summ_pass else hin[c][:, j:j + 1]
                dv.inc(ve.tensor_tensor_scan(Hloc[:, sl], Abuf[:, sl],
                                             Vbuf[:, sl], init,
                                             ALU.mult, ALU.add))
                if summ_pass:
                    dv.inc(ve.tensor_tensor_scan(Ploc[:, sl], Abuf[:, sl],
                                                 zeros[:], 1.0,
                                                 ALU.mult, ALU.add))
            ve.wait_ge(dv.h, dv.n)
            if summ_pass:
                for j in range(NT):
                    dv.inc(ve.tensor_copy(summ[:, 8 + j:9 + j],
                                          Hloc[:, j * 128 + 127:j * 128 + 128]))
                    dv.inc(ve.tensor_copy(summ[:, j:j + 1],
                                          Ploc[:, j * 128 + 127:j * 128 + 128]))
            else:
                dv.inc(ve.tensor_copy(hbf[c][:], Hloc[:]))
            ve.wait_ge(dv.h, dv.n)
        dv_done = dv.n

        if summ_pass:
            @blk.sync
            def _(sy, c=c, dv_done=dv_done):
                sy.wait_ge(dv.h, dv_done)
                qs.inc(sy.dma_start(out=summ_out[c * 128:(c + 1) * 128, :],
                                    in_=summ[:]), 16)
        if summ_pass:
            qs_s = qs.n

            @blk.vector
            def _(ve, qs_s=qs_s):
                ve.wait_ge(qs.h, qs_s)
    return None


def emit_l1a(k: K, standalone=True, stop_after=None):
    nc, blk = k.nc, k.block
    qs, qp, pe, dv, ac, po = k.qs, k.qp, k.pe, k.dv, k.ac, k.po

    x1_tm = k.dram("x1_tm", [TOK, D], F32, "ExternalInput")
    whg = k.dram("whg", [D, 2 * DI], F32, "ExternalInput")
    xn2_out = k.dram("xn2_out", [D, TOK], BF16, "ExternalOutput")
    summ_out = k.dram("summ_out", [DI, 16], F32, "ExternalOutput")

    if standalone:
        k.emit_identity()

    k.xn2 = [k.sbuf(f"xn2_{kk}", [128, TOK], BF16) for kk in range(DC)]
    xf = [k.sbuf(f"l1_xf{i}", [128, D], F32) for i in range(2)]
    xn_tm = k.sbuf("l1_xn_tm", [128, D], BF16)
    mv6 = k.sbuf("l1_mv6", [128, 24], F32)
    mv2 = k.sbuf("l1_mv2", [128, 2], F32)
    stat = k.sbuf("l1_stat", [128, 2], F32)
    xn_done = {}
    xnT_ev = {}

    for j in range(NT):
        @blk.sync
        def _(sy, j=j):
            if j >= 2:
                sy.wait_ge(dv.h, xn_done[j - 2])
            qs.inc(sy.dma_start(out=xf[j % 2][:],
                                in_=x1_tm[j * 128:(j + 1) * 128, :]), 16)

        qs_l = qs.n

        @blk.vector
        def _(ve, j=j, qs_l=qs_l):
            ve.wait_ge(qs.h, qs_l)
            for gq in range(4):
                dv.inc(ve.bn_stats(mv6[:, 6 * gq:6 * gq + 6],
                                   xf[j % 2][:, 512 * gq:512 * (gq + 1)]))
            ve.wait_ge(dv.h, dv.n)
            dv.inc(ve.bn_aggr(mv2[:], mv6[:]))
            ve.wait_ge(dv.h, dv.n)
            dv.inc(ve.tensor_scalar_add(stat[:, 1:2], mv2[:, 1:2], LN_EPS))
            dv.inc(ve.tensor_copy(stat[:, 0:1], mv2[:, 0:1]))
            ve.wait_ge(dv.h, dv.n)
            ve.wait_ge(ac.h, ac.n + 1)
            dv.inc(ve.reciprocal(stat[:, 1:2], stat[:, 1:2]))
            ve.wait_ge(dv.h, dv.n)
            if j >= 1:
                ve.wait_ge(ac.h, xnT_ev[j - 1])
            dv.inc(ve.tensor_scalar(xn_tm[:], xf[j % 2][:], stat[:, 0:1],
                                    stat[:, 1:2], ALU.subtract, ALU.mult))
            xn_done[j] = dv.n

        @blk.scalar
        def _(sc, j=j):
            sc.wait_ge(dv.h, dv.n - 2)
            ac.inc(sc.activation(stat[:, 1:2], stat[:, 1:2], AF.Sqrt))

        dv_xn = dv.n
        pe_t0, ac_t0 = pe.n, ac.n

        @blk.tensor
        def _(t, j=j, dv_xn=dv_xn, ac_t0=ac_t0):
            t.wait_ge(dv.h, dv_xn)
            for grp in range(4):
                if grp >= 1:
                    t.wait_ge(ac.h, ac_t0 + grp)
                for s4 in range(4):
                    kk = grp * 4 + s4
                    i = t.transpose(k.pt_bf[:, s4 * 128:(s4 + 1) * 128],
                                    xn_tm[:, kk * 128:(kk + 1) * 128], k.ident[:])
                pe.inc(i)

        @blk.scalar
        def _(sc, j=j, pe_t0=pe_t0):
            for grp in range(4):
                sc.wait_ge(pe.h, pe_t0 + grp + 1)
                for s4 in range(4):
                    kk = grp * 4 + s4
                    i = sc.copy(k.xn2[kk][:, j * 128:(j + 1) * 128],
                                k.pt_bf[:, s4 * 128:(s4 + 1) * 128])
                ac.inc(i)
            xnT_ev[j] = ac.n

    k.barrier()

    @blk.sync
    def _(sy):
        for kk in range(DC):
            qs.inc(sy.dma_start(out=xn2_out[kk * 128:(kk + 1) * 128, :],
                                in_=k.xn2[kk][:]), 16)

    k.barrier()
    if stop_after == "xn2":
        return None

    emit_l1_scan_chunks(k, whg, summ_pass=True, summ_out=summ_out, sfx="a")
    k.barrier()
    return None


def emit_l1b(k: K, standalone=True, stop_after=None):
    nc, blk = k.nc, k.block
    qs, qp, pe, dv, ac, po = k.qs, k.qp, k.pe, k.dv, k.ac, k.po

    xn2_in = k.dram("xn2_in", [D, TOK], BF16, "ExternalInput")
    x1_tm = k.dram("x1_tm", [TOK, D], F32, "ExternalInput")
    summ_all = k.dram("summ_all", [DI, 128], F32, "ExternalInput")
    hsel_in = k.dram("hsel_in", [128, 8 * 64], F32, "ExternalInput")
    whg = k.dram("whg", [D, 2 * DI], F32, "ExternalInput")
    w_out = k.dram("w_out", [DI, D], F32, "ExternalInput")
    wr_in = k.dram("wr_in", [D, 8], F32, "ExternalInput")
    we_in = k.dram("we_in", [8, D, D], F32, "ExternalInput")
    bebc_in = k.dram("bebc_in", [8, 128, D], F32, "ExternalInput")
    y_out = k.dram("y_out", [TOK, D], F32, "ExternalOutput")
    s_dram = k.dram("s_scratch", [NT, 128, D], BF16, "Internal")

    if standalone:
        k.emit_identity()
    k.xn2 = [k.sbuf(f"xn2_{kk}", [128, TOK], BF16) for kk in range(DC)]

    @blk.sync
    def _(sy):
        for kk in range(DC):
            qs.inc(sy.dma_start(out=k.xn2[kk][:],
                                in_=xn2_in[kk * 128:(kk + 1) * 128, :]), 16)

    sust = k.sbuf("l1b_sust", [128, 128], F32)
    hsh = k.sbuf("l1b_hsh", [128, 64], F32)
    hsel = k.sbuf("l1b_hsel", [128, 8 * 64], F32)
    hdot = k.sbuf("l1b_hdot", [128, 64], F32)
    hin = [k.sbuf(f"l1b_hin{c}", [128, NT], F32) for c in range(EC)]

    @blk.sync
    def _(sy):
        qs.inc(sy.dma_start(out=hsel[:], in_=hsel_in), 16)

    k.barrier()
    sudone = {}
    for c in range(EC):
        @blk.sync
        def _(sy, c=c):
            if c >= 1:
                sy.wait_ge(dv.h, sudone[c - 1])
            qs.inc(sy.dma_start(out=sust[:],
                                in_=summ_all[c * 128:(c + 1) * 128, :]), 16)

        qs_su = qs.n

        @blk.vector
        def _(ve, c=c, qs_su=qs_su):
            ve.wait_ge(qs.h, qs_su)
            dv.inc(ve.tensor_tensor_scan(hsh[:], sust[:, 0:64],
                                         sust[:, 64:128], 0.0,
                                         ALU.mult, ALU.add))
            ve.wait_ge(dv.h, dv.n)
            dv.inc(ve.tensor_copy(sust[:, 1:64], hsh[:, 0:63]))
            ve.wait_ge(dv.h, dv.n)
            dv.inc(ve.memset(sust[:, 0:1], 0.0))
            ve.wait_ge(dv.h, dv.n)
            for j in range(NT):
                dv.inc(ve.tensor_mul(hdot[:], sust[:, 0:64],
                                     hsel[:, j * 64:(j + 1) * 64]))
                ve.wait_ge(dv.h, dv.n)
                dv.inc(ve.reduce_sum(hin[c][:, j:j + 1], hdot[:], axis=AX.X))
                ve.wait_ge(dv.h, dv.n)
            sudone[c] = dv.n

    k.barrier()
    if stop_after == "hin":
        hdbg = k.dram("hin_dbg", [EC, 128, NT], F32, "ExternalOutput")

        @blk.sync
        def _(sy):
            for c in range(EC):
                qs.inc(sy.dma_start(out=hdbg[c], in_=hin[c][:]), 16)
        k.barrier()
        return None

    hbf = [k.sbuf(f"l1b_hbf{c}", [128, TOK], BF16) for c in range(EC)]
    k._l1b_ar = k.arena("l1b_arena", 98304)
    emit_l1_scan_chunks(k, whg, summ_pass=False, hin=hin, hbf=hbf, sfx="b",
                        at=k._l1b_ar)
    k.barrier()
    if stop_after == "H":
        hdbg2 = k.dram("H_dbg", [DI, TOK], BF16, "ExternalOutput")

        @blk.sync
        def _(sy):
            for c in range(EC):
                qs.inc(sy.dma_start(out=hdbg2[c * 128:(c + 1) * 128, :],
                                    in_=hbf[c][:]), 16)
        k.barrier()
        return None

    wout_res = [k.sbuf_at(f"l1b_wo{ec}", [128, D], BF16,
                          k._l1b_ar + 4096 * ec) for ec in range(EC)]
    s_stg = [k.sbuf(f"l1b_sst{i}", [128, D], BF16) for i in range(2)]

    @blk.gpsimd
    def _(g):
        for ec in range(EC):
            qp.inc(g.dma_start(out=wout_res[ec][:],
                               in_=w_out[ec * 128:(ec + 1) * 128, :]), 16)

    k.barrier()
    sdma = {}
    for qt in range(NT):
        @blk.tensor
        def _(t, qt=qt):
            if qt >= 1:
                t.wait_ge(ac.h, sev[qt - 1])
            for grp in range(4):
                for ec in range(EC):
                    i = t.matmul(k.pp[grp][:],
                                 hbf[ec][:, qt * 128:(qt + 1) * 128],
                                 wout_res[ec][:, grp * 512:(grp + 1) * 512],
                                 start=(ec == 0), stop=(ec == EC - 1))
                pe.inc(i)

        pe_s = pe.n
        if qt == 0:
            sev = {}

        @blk.scalar
        def _(sc, qt=qt, pe_s=pe_s):
            sc.wait_ge(pe.h, pe_s)
            if qt >= 2:
                sc.wait_ge(qs.h, sdma[qt - 2])
            for grp in range(4):
                i = sc.copy(s_stg[qt % 2][:, grp * 512:(grp + 1) * 512],
                            k.pp[grp][:])
            ac.inc(i)
        sev[qt] = ac.n

        @blk.sync
        def _(sy, qt=qt, an=ac.n):
            sy.wait_ge(ac.h, an)
            qs.inc(sy.dma_start(out=s_dram[qt], in_=s_stg[qt % 2][:]), 16)
            sdma[qt] = qs.n

    k.barrier()

    wr_sb = [k.sbuf(f"l1b_wr{dc}", [128, 8], BF16) for dc in range(DC)]
    lg = k.sbuf("l1b_lg", [128, 8], F32)
    m1 = k.sbuf("l1b_m1", [128, 1], F32)
    m2 = k.sbuf("l1b_m2", [128, 1], F32)
    eq1 = k.sbuf("l1b_eq1", [128, 8], F32)
    eq2 = k.sbuf("l1b_eq2", [128, 8], F32)
    g1 = k.sbuf("l1b_g1", [128, 1], F32)
    gate_all = k.sbuf("l1b_gate", [128, 8 * NT], F32)
    inv128 = k.sbuf("l1b_inv128", [128, 128], BF16)
    bb_sb = k.sbuf("l1b_bb", [128, D], BF16)
    yacc = [k.sbuf_at(f"l1b_yacc{qt}", [128, D], BF16,
                      k._l1b_ar + 65536 + 4096 * qt) for qt in range(NT)]
    ygt = k.sbuf("l1b_ygt", [128, D], BF16)

    @blk.gpsimd
    def _(g):
        g.memset(inv128[:], 1.0 / 128.0)
        for dc in range(DC):
            qp.inc(g.dma_start(out=wr_sb[dc][:],
                               in_=wr_in[dc * 128:(dc + 1) * 128, :]), 16)
        for qt in range(NT - 1):
            g.memset(yacc[qt][:], 0.0)
        po.inc(g.memset(yacc[NT - 1][:], 0.0))

    k.barrier()
    for qt in range(NT):
        @blk.tensor
        def _(t, qt=qt):
            if qt >= 1:
                t.wait_ge(dv.h, gdone[qt - 1])
            for dc in range(DC):
                i = t.matmul(k.pt32[:, 0:8],
                             k.xn2[dc][:, qt * 128:(qt + 1) * 128],
                             wr_sb[dc][:], start=(dc == 0), stop=(dc == DC - 1))
            pe.inc(i)

        pe_lg = pe.n
        if qt == 0:
            gdone = {}

        @blk.vector
        def _(ve, qt=qt, pe_lg=pe_lg):
            ve.wait_ge(pe.h, pe_lg)
            dv.inc(ve.tensor_copy(lg[:], k.pt32[:, 0:8]))
            ve.wait_ge(dv.h, dv.n)
            dv.inc(ve.reduce_max(m1[:], lg[:], axis=AX.X))
            ve.wait_ge(dv.h, dv.n)
            dv.inc(ve.tensor_scalar(eq1[:], lg[:], m1[:], None, ALU.is_equal))
            ve.wait_ge(dv.h, dv.n)
            dv.inc(ve.scalar_tensor_tensor(eq2[:], eq1[:], NEG, lg[:],
                                           ALU.mult, ALU.add))
            ve.wait_ge(dv.h, dv.n)
            dv.inc(ve.reduce_max(m2[:], eq2[:], axis=AX.X))
            ve.wait_ge(dv.h, dv.n)
            dv.inc(ve.tensor_scalar(eq2[:], eq2[:], m2[:], None, ALU.is_equal))
            dv.inc(ve.tensor_sub(m1[:], m1[:], m2[:]))
            ve.wait_ge(dv.h, dv.n)
            ve.wait_ge(ac.h, ac.n + 1)
            dv.inc(ve.tensor_scalar(eq1[:], eq1[:], g1[:], None, ALU.mult))
            ve.wait_ge(dv.h, dv.n)
            dv.inc(ve.tensor_scalar(g1[:], g1[:], -1.0, 1.0, ALU.mult, ALU.add))
            ve.wait_ge(dv.h, dv.n)
            dv.inc(ve.tensor_scalar(eq2[:], eq2[:], g1[:], None, ALU.mult))
            ve.wait_ge(dv.h, dv.n)
            dv.inc(ve.tensor_add(gate_all[:, qt * 8:(qt + 1) * 8],
                                 eq1[:], eq2[:]))
            gdone[qt] = dv.n

        @blk.scalar
        def _(sc, qt=qt):
            sc.wait_ge(dv.h, dv.n - 4)
            ac.inc(sc.activation(g1[:], m1[:], AF.Sigmoid))

    k.barrier()
    if stop_after == "gates":
        gdbg = k.dram("gates_dbg", [128, 8 * NT], F32, "ExternalOutput")

        @blk.sync
        def _(sy):
            qs.inc(sy.dma_start(out=gdbg, in_=gate_all[:]), 16)
        k.barrier()
        return None

    we_res = [k.sbuf_at(f"l1b_we{dc}", [128, D], BF16,
                        k.nc.lookup_mloc(wout_res[dc]).addr) for dc in range(DC)]
    ldwe = {}
    emm = {}
    emmq = {}
    yev = {}
    s_ya = Sem(nc)
    for e in range(8):
        @blk.gpsimd
        def _(g, e=e):
            if e >= 1:
                g.wait_ge(pe.h, emm[e - 1])
            for dc in range(DC):
                qp.inc(g.dma_start(out=we_res[dc][:], in_=we_in[e][dc * 128:(dc + 1) * 128, :]), 16)
            qp.inc(g.dma_start(out=bb_sb[:], in_=bebc_in[e]), 16)
            ldwe[e] = qp.n

        @blk.tensor
        def _(t, e=e):
            t.wait_ge(qp.h, ldwe[e])
            for qt in range(NT):
                if e * NT + qt >= 1:
                    t.wait_ge(s_ya.h, e * NT + qt)
                for grp in range(4):
                    for dc in range(DC):
                        i = t.matmul(k.pp[grp + (qt % 2) * 0][:],
                                     k.xn2[dc][:, qt * 128:(qt + 1) * 128],
                                     we_res[dc][:, grp * 512:(grp + 1) * 512],
                                     start=(dc == 0), stop=False)
                    i = t.matmul(k.pp[grp][:], inv128[:],
                                 bb_sb[:, grp * 512:(grp + 1) * 512],
                                 start=False, stop=True)
                pe.inc(i)
                emmq[(e, qt)] = pe.n
        emm[e] = pe.n

        @blk.scalar
        def _(sc, e=e):
            for qt in range(NT):
                sc.wait_ge(pe.h, emmq[(e, qt)])
                if e * NT + qt >= 1:
                    sc.wait_ge(s_ya.h, e * NT + qt)
                for grp in range(4):
                    i = sc.activation(ygt[:, grp * 512:(grp + 1) * 512],
                                      k.pp[grp][:], AF.Copy,
                                      scale=gate_all[:, qt * 8 + e:qt * 8 + e + 1])
                ac.inc(i)
                yev[(e, qt)] = ac.n

        @blk.vector
        def _(ve, e=e):
            for qt in range(NT):
                ve.wait_ge(ac.h, yev[(e, qt)])
                i = ve.tensor_add(yacc[qt][:], yacc[qt][:], ygt[:])
                s_ya.inc(i)

    k.barrier()

    xstg2 = [k.sbuf_at(f"l1b_x{i}", [128, D], F32, k._l1b_ar + 8192 * i)
             for i in range(2)]
    tprod = k.sbuf_at("l1b_tp", [128, D], F32, k._l1b_ar + 16384)
    cmb = {}
    for qt in range(NT):
        @blk.sync
        def _(sy, qt=qt):
            if qt >= 2:
                sy.wait_ge(dv.h, cmb[qt - 2])
            qs.inc(sy.dma_start(out=xstg2[qt % 2][:],
                                in_=x1_tm[qt * 128:(qt + 1) * 128, :]), 16)
            qs.inc(sy.dma_start(out=s_stg[qt % 2][:], in_=s_dram[qt]), 16)

        qs_c = qs.n

        @blk.scalar
        def _(sc, qt=qt):
            ac.inc(sc.activation(yacc[qt][:], yacc[qt][:], AF.Silu))

        ac_c = ac.n

        @blk.vector
        def _(ve, qt=qt, qs_c=qs_c, ac_c=ac_c):
            ve.wait_ge(qs.h, qs_c)
            ve.wait_ge(ac.h, ac_c)
            dv.inc(ve.tensor_mul(tprod[:], s_stg[qt % 2][:], yacc[qt][:]))
            ve.wait_ge(dv.h, dv.n)
            dv.inc(ve.tensor_add(xstg2[qt % 2][:], xstg2[qt % 2][:], tprod[:]))
            cmb[qt] = dv.n

        @blk.sync
        def _(sy, qt=qt, dn=None):
            sy.wait_ge(dv.h, cmb[qt])
            qs.inc(sy.dma_start(out=y_out[qt * 128:(qt + 1) * 128, :],
                                in_=xstg2[qt % 2][:]), 16)

    k.barrier()
    return None


def build_pre(dbg=False, stop_after=None):
    k = K()
    emit_pre(k, dbg=dbg, stop_after=stop_after)
    nc = k.finish()
    return nc


_PROGRAMS = {}


def _get_program(name):
    if name in _PROGRAMS:
        return _PROGRAMS[name]
    pools = {"attn": "attn", "ffn": "ffn"}
    k = K(pool=pools.get(name, "default"))
    if name == "pre":
        emit_pre(k, dbg=True)
    elif name == "attn":
        emit_attn(k, standalone=True)
    elif name == "ffn":
        emit_ffn(k, standalone=True)
    elif name == "l1a":
        emit_l1a(k, standalone=True)
    elif name == "l1b":
        emit_l1b(k, standalone=True)
    nc = k.finish()
    _PROGRAMS[name] = nc
    return nc


def _own_rows(r):
    idx = []
    for j in range(NT):
        g = r + 8 * j
        idx.extend(range(g * 128, (g + 1) * 128))
    return np.array(idx)


def _core_mask(r):
    m = np.zeros((128, 1024), np.float32)
    qi = np.arange(128)[:, None]
    ki = np.arange(128)[None, :]
    for lt in range(8):
        if lt == r:
            m[:, lt * 128:(lt + 1) * 128] = np.where(ki <= qi, 0.0, NEG)
        elif lt > r:
            m[:, lt * 128:(lt + 1) * 128] = NEG
    return m


def _hsel(r):
    m = np.zeros((128, 8 * 64), np.float32)
    for j in range(NT):
        g = r + 8 * j
        m[:, j * 64 + g] = 1.0
    return m


def kernel(L0_attn_x, L0_attn_W_q, L0_attn_Wd_kv, L0_attn_W_uv, L0_attn_W_uk,
           L0_attn_W_o, L0_attn_gamma, L0_attn_beta, L0_ffn_w1, L0_ffn_w2,
           L0_ffn_w3, L1_attn_w_hg, L1_attn_w_out, L1_ffn_Wr, L1_ffn_We,
           L1_ffn_be):
    bf16 = ml_dtypes.bfloat16
    f32 = np.float32
    x = np.ascontiguousarray(np.asarray(L0_attn_x, f32)[0])
    rows = [_own_rows(r) for r in range(NCORE)]
    xo = [np.ascontiguousarray(x[rows[r]]) for r in range(NCORE)]
    xoT = [np.ascontiguousarray(xo[r].T) for r in range(NCORE)]

    nc = _get_program("pre")
    gbc = np.ascontiguousarray(np.broadcast_to(
        np.asarray(L0_attn_gamma, f32)[None, :], (128, LAT)))
    bbc = np.ascontiguousarray(np.broadcast_to(
        np.asarray(L0_attn_beta, f32)[None, :], (128, LAT)))
    in_maps = [{
        "x_full": x, "x_own": xo[r],
        "wdkv": np.asarray(L0_attn_Wd_kv, f32),
        "wuv": np.asarray(L0_attn_W_uv, f32),
        "gamma_bc": gbc, "beta_bc": bbc,
    } for r in range(NCORE)]
    res = run_bass_kernel_spmd(nc, in_maps, core_ids=list(range(NCORE)))
    v_full = res.results[0]["v_dram"]
    ckv = res.results[0]["ckv_out"]
    xn = [res.results[r]["xn_out"] for r in range(NCORE)]

    nc = _get_program("attn")
    in_maps = [{
        "wq": np.asarray(L0_attn_W_q, f32),
        "wuk": np.asarray(L0_attn_W_uk, f32),
        "wo": np.asarray(L0_attn_W_o, f32),
        "maskin": _core_mask(r),
        "v_dram": v_full, "xn_in": xn[r], "ckv_in": ckv,
    } for r in range(NCORE)]
    res = run_bass_kernel_spmd(nc, in_maps, core_ids=list(range(NCORE)))
    satt = [res.results[r]["satt_out"] for r in range(NCORE)]

    nc = _get_program("ffn")
    in_maps = [{
        "w1": np.asarray(L0_ffn_w1, f32), "w2": np.asarray(L0_ffn_w2, f32),
        "w3": np.asarray(L0_ffn_w3, f32),
        "x_ownT": xoT[r], "xn_in": xn[r], "satt_in": satt[r],
    } for r in range(NCORE)]
    res = run_bass_kernel_spmd(nc, in_maps, core_ids=list(range(NCORE)))
    x1T = [res.results[r]["x1_out"] for r in range(NCORE)]
    x1tm = [np.ascontiguousarray(x1T[r].T) for r in range(NCORE)]

    nc = _get_program("l1a")
    whg = np.asarray(L1_attn_w_hg, f32)
    in_maps = [{"x1_tm": x1tm[r], "whg": whg} for r in range(NCORE)]
    res = run_bass_kernel_spmd(nc, in_maps, core_ids=list(range(NCORE)))
    xn2 = [res.results[r]["xn2_out"] for r in range(NCORE)]
    summ_all = np.empty((DI, 128), np.float32)
    for r in range(NCORE):
        su = res.results[r]["summ_out"]
        for j in range(NT):
            g = r + 8 * j
            summ_all[:, g] = su[:, j]
            summ_all[:, 64 + g] = su[:, 8 + j]

    nc = _get_program("l1b")
    bebc = np.ascontiguousarray(np.broadcast_to(
        np.asarray(L1_ffn_be, f32)[:, None, :], (8, 128, D)))
    in_maps = [{
        "xn2_in": xn2[r], "x1_tm": x1tm[r], "summ_all": summ_all,
        "hsel_in": _hsel(r), "whg": whg,
        "w_out": np.asarray(L1_attn_w_out, f32),
        "wr_in": np.asarray(L1_ffn_Wr, f32),
        "we_in": np.asarray(L1_ffn_We, f32),
        "bebc_in": bebc,
    } for r in range(NCORE)]
    res = run_bass_kernel_spmd(nc, in_maps, core_ids=list(range(NCORE)))

    y = np.empty((S, D), np.float32)
    for r in range(NCORE):
        y[rows[r]] = res.results[r]["y_out"]
    return y.reshape(1, S, D)
